# version 40
# baseline (speedup 1.0000x reference)
import contextlib
import numpy as np
import ml_dtypes
import concourse.bass as bass
import concourse.mybir as mybir
from concourse.bass_utils import run_bass_kernel_spmd

F32 = mybir.dt.float32
BF16 = mybir.dt.bfloat16
I32 = mybir.dt.int32
AF = mybir.ActivationFunctionType
ALU = mybir.AluOpType
AX = mybir.AxisListType

MULT = ALU.mult
ADD = ALU.add


def fp_of(ap):
    lo = ap.offset
    hi = ap.offset
    for st, ct in ap.ap:
        d = st * (ct - 1)
        if d < 0:
            lo += d
        else:
            hi += d
    return (ap.tensor.name, lo, hi + 1)


class P:

    CENG = ("t", "v", "s", "g")
    NSUB = 16
    QENG = tuple(f"qs{i}" for i in range(16)) + tuple(f"qg{i}" for i in range(16))

    def __init__(self):
        self.nc = bass.Bass("TRN2", target_bir_lowering=False)
        self.es = contextlib.ExitStack()
        self.stacks = [self.es]
        self.sem = {}
        self.cnt = {}
        for e in self.CENG + self.QENG:
            self.sem[e] = self.es.enter_context(self.nc.semaphore("sem_" + e))
            self.cnt[e] = 0
        self.qrr = {"qs": 0, "qg": 0}
        self.waited = {e: {} for e in self.CENG + ("sy",)}
        self.rec = {}
        self.pe_pending = []
        self.n_inst = 0
        self._names = set()

    def _uniq(self, name):
        if name in self._names:
            k = 2
            while f"{name}_{k}" in self._names:
                k += 1
            name = f"{name}_{k}"
        self._names.add(name)
        return name

    def sb(self, name, shape, dtype):
        name = self._uniq(name)
        return self.stacks[-1].enter_context(self.nc.sbuf_tensor(name, list(shape), dtype))

    def ps(self, name, shape, dtype):
        return self.stacks[-1].enter_context(self.nc.psum_tensor(name, list(shape), dtype))

    def push(self):
        self.stacks.append(contextlib.ExitStack())

    def pop(self):
        self.stacks.pop().close()

    def barrier(self):
        for e in ("t", "v", "s", "g", "sy"):
            for sn in self.CENG + self.QENG:
                if self.cnt[sn] and not (e == sn):
                    self._wait(e, sn, self.cnt[sn])

    def dram(self, name, shape, dtype, kind=None):
        if kind:
            return self.nc.dram_tensor(name, list(shape), dtype, kind=kind).ap()
        return self.nc.dram_tensor(name, list(shape), dtype).ap()

    def _engine_obj(self, e):
        return {"t": self.nc.tensor, "v": self.nc.vector, "s": self.nc.scalar,
                "g": self.nc.gpsimd, "sy": self.nc.sync}[e]

    def _wait(self, eng, sname, val):
        w = self.waited[eng]
        if w.get(sname, 0) >= val:
            return
        self._engine_obj(eng).wait_ge(self.sem[sname], val)
        w[sname] = val

    def _dep(self, eng, reads, writes):
        own = {"t": "t", "v": "v", "s": "s", "g": "g", "sy": None}[eng]
        for ap in reads:
            nm, lo, hi = fp_of(ap)
            r = self.rec.get(nm)
            if not r:
                continue
            for (l2, h2, sname, val) in r["w"]:
                if l2 < hi and lo < h2:
                    if eng == "t" and sname == "t":
                        continue
                    self._wait(eng, sname, val)
        for ap in writes:
            nm, lo, hi = fp_of(ap)
            r = self.rec.get(nm)
            if not r:
                continue
            for (l2, h2, sname, val) in r["w"]:
                if l2 < hi and lo < h2:
                    if eng == "t" and sname == "t":
                        continue
                    self._wait(eng, sname, val)
            for (l2, h2, sname, val) in r["r"]:
                if l2 < hi and lo < h2:
                    if eng == "t" and sname == "t":
                        continue
                    self._wait(eng, sname, val)

    def _rec_access(self, kind, nm, lo, hi, sname, val):
        r = self.rec.setdefault(nm, {"w": [], "r": []})
        lst = r[kind]
        if kind == "w":
            r["w"] = [x for x in r["w"] if not (lo <= x[0] and x[1] <= hi)]
            r["r"] = [x for x in r["r"] if not (lo <= x[0] and x[1] <= hi)]
            lst = r["w"]
        lst.append((lo, hi, sname, val))
        if len(lst) > 96:
            del lst[0:32]

    def _commit(self, eng, inst, reads, writes, inc=True):
        self.n_inst += 1
        if eng in ("t",):
            if inc:
                self.cnt["t"] += 1
                inst.then_inc(self.sem["t"], 1)
                val = self.cnt["t"]
                for (kind, nm, lo, hi) in self.pe_pending:
                    self._rec_access(kind, nm, lo, hi, "t", val)
                self.pe_pending = []
                for ap in reads:
                    nm, lo, hi = fp_of(ap)
                    self._rec_access("r", nm, lo, hi, "t", val)
                for ap in writes:
                    nm, lo, hi = fp_of(ap)
                    self._rec_access("w", nm, lo, hi, "t", val)
            else:
                for ap in reads:
                    nm, lo, hi = fp_of(ap)
                    self.pe_pending.append(("r", nm, lo, hi))
                for ap in writes:
                    nm, lo, hi = fp_of(ap)
                    self.pe_pending.append(("w", nm, lo, hi))
        else:
            self.cnt[eng] += 1
            inst.then_inc(self.sem[eng], 1)
            val = self.cnt[eng]
            for ap in reads:
                nm, lo, hi = fp_of(ap)
                self._rec_access("r", nm, lo, hi, eng, val)
            for ap in writes:
                nm, lo, hi = fp_of(ap)
                self._rec_access("w", nm, lo, hi, eng, val)

    def mm(self, out, lhsT, rhs, start=True, stop=True, tp=None, commit=True):
        self._dep("t", [lhsT, rhs], [out])
        kw = {}
        if tp is not None:
            kw["tile_position"] = tp
        inst = self.nc.tensor.matmul(out, lhsT, rhs, start=start, stop=stop, **kw)
        self._commit("t", inst, [lhsT, rhs], [out], inc=commit)

    def trans(self, out_ps, in_sb, ident, commit=True):
        self._dep("t", [in_sb, ident], [out_ps])
        inst = self.nc.tensor.transpose(out_ps, in_sb, ident)
        self._commit("t", inst, [in_sb, ident], [out_ps], inc=commit)

    def act(self, out, in_, func, bias=0.0, scale=1.0, accum_out=None):
        reads = [in_] + [x for x in (bias, scale) if isinstance(x, bass.AP)]
        writes = [out] + ([accum_out] if accum_out is not None else [])
        self._dep("s", reads, writes)
        inst = self.nc.scalar.activation(out, in_, func, bias=bias, scale=scale,
                                         accum_out=accum_out)
        self._commit("s", inst, reads, writes)

    def tt(self, out, a, b, op, eng="v"):
        self._dep(eng, [a, b], [out])
        inst = self._engine_obj(eng).tensor_tensor(out, a, b, op)
        self._commit(eng, inst, [a, b], [out])

    def ts(self, out, in0, s1, op0, s2=None, op1=ALU.bypass, eng="v", accum_out=None):
        reads = [in0] + [x for x in (s1, s2) if isinstance(x, bass.AP)]
        writes = [out] + ([accum_out] if accum_out is not None else [])
        self._dep(eng, reads, writes)
        inst = self._engine_obj(eng).tensor_scalar(out, in0, s1, s2, op0, op1,
                                                   accum_out=accum_out)
        self._commit(eng, inst, reads, writes)

    def stt(self, out, in0, scalar, in1, op0, op1, eng="v"):
        reads = [in0, in1] + ([scalar] if isinstance(scalar, bass.AP) else [])
        self._dep(eng, reads, [out])
        inst = self._engine_obj(eng).scalar_tensor_tensor(out, in0, scalar, in1, op0, op1)
        self._commit(eng, inst, reads, [out])

    def scan(self, out, d0, d1, initial=0.0):
        reads = [d0, d1] + ([initial] if isinstance(initial, bass.AP) else [])
        self._dep("v", reads, [out])
        inst = self.nc.vector.tensor_tensor_scan(out, d0, d1, initial, MULT, ADD)
        self._commit("v", inst, reads, [out])

    def evac(self, out, psum, bias=None, scale=None, eng="s"):
        if eng == "s":
            kw = {}
            if bias is not None:
                kw["bias"] = bias
            if scale is not None:
                kw["scale"] = scale
            return self.act(out, psum, AF.Identity, **kw)
        if scale is not None and bias is not None:
            return self.ts(out, psum, scale, MULT, s2=bias, op1=ADD, eng=eng)
        if scale is not None:
            return self.ts(out, psum, scale, MULT, eng=eng)
        if bias is not None:
            return self.ts(out, psum, bias, ADD, eng=eng)
        return self.copy(out, psum, eng=eng)

    def copy(self, out, in_, eng="v"):
        if eng == "s":
            return self.act(out, in_, AF.Identity)
        self._dep(eng, [in_], [out])
        inst = self._engine_obj(eng).tensor_copy(out, in_)
        self._commit(eng, inst, [in_], [out])

    def red(self, out, in_, op, eng="v"):
        self._dep(eng, [in_], [out])
        inst = self._engine_obj(eng).tensor_reduce(out, in_, AX.X, op)
        self._commit(eng, inst, [in_], [out])

    def bnstats(self, out, in_):
        self._dep("v", [in_], [out])
        inst = self.nc.vector.bn_stats(out, in_)
        self._commit("v", inst, [in_], [out])

    def bnaggr(self, out, in_):
        self._dep("v", [in_], [out])
        inst = self.nc.vector.bn_aggr(out, in_)
        self._commit("v", inst, [in_], [out])

    def recip(self, out, in_):
        self._dep("v", [in_], [out])
        inst = self.nc.vector.reciprocal(out, in_)
        self._commit("v", inst, [in_], [out])

    def memset(self, ap, c, eng="g"):
        self._dep(eng, [], [ap])
        inst = self._engine_obj(eng).memset(ap, c)
        self._commit(eng, inst, [], [ap])

    def dma(self, out, in_, q="qs"):
        eng = {"qs": "sy", "qg": "g"}[q]
        sub = f"{q}{self.qrr[q]}"
        self.qrr[q] = (self.qrr[q] + 1) % self.NSUB
        self._wait(eng, sub, self.cnt[sub])
        self._dep(eng, [in_], [out])
        inst = self._engine_obj(eng).dma_start(out, in_)
        self.cnt[sub] += 16
        inst.then_inc(self.sem[sub], 16)
        val = self.cnt[sub]
        nm, lo, hi = fp_of(in_)
        self._rec_access("r", nm, lo, hi, sub, val)
        nm, lo, hi = fp_of(out)
        self._rec_access("w", nm, lo, hi, sub, val)
        self.n_inst += 1

    def drain(self):
        for e in self.CENG:
            if self.cnt[e]:
                self._wait("g", e, self.cnt[e])
        for q in self.QENG:
            if self.cnt[q]:
                self._wait("g", q, self.cnt[q])



def host_consts():
    cc = np.zeros((128, 8), np.float32)
    cc[:, 0] = 1e-5
    cc[:, 1] = 0.125
    cc[:, 2] = 1.0
    cc[:, 3] = 1.0 / 127.0
    idn = np.eye(128, dtype=ml_dtypes.bfloat16)
    ones = np.ones((128, 128), dtype=ml_dtypes.bfloat16)
    eye64 = np.tile(np.eye(64, dtype=ml_dtypes.bfloat16), (2, 1))
    onesf = np.ones((1, 128), np.float32)
    return {"cc": cc, "idn": idn, "onesb": ones, "eye64": eye64, "onesf": onesf}


def decl_consts(p):
    d = {}
    d["cc_d"] = p.dram("cc", [128, 8], F32, kind="ExternalInput")
    d["idn_d"] = p.dram("idn", [128, 128], BF16, kind="ExternalInput")
    d["ones_d"] = p.dram("onesb", [128, 128], BF16, kind="ExternalInput")
    d["eye64_d"] = p.dram("eye64", [128, 64], BF16, kind="ExternalInput")
    d["onesf_d"] = p.dram("onesf", [1, 128], F32, kind="ExternalInput")
    d["cc"] = p.sb("cc_sb", (128, 8), F32)
    d["idn"] = p.sb("idn_sb", (128, 128), BF16)
    d["ones"] = p.sb("ones_sb", (128, 128), BF16)
    d["eye64"] = p.sb("eye64_sb", (128, 64), BF16)
    d["onesf"] = p.sb("onesf_sb", (1, 128), F32)
    p.dma(d["cc"][:], d["cc_d"], q="qs")
    p.dma(d["idn"][:], d["idn_d"], q="qs")
    p.dma(d["ones"][:], d["ones_d"], q="qs")
    p.dma(d["eye64"][:], d["eye64_d"], q="qs")
    p.dma(d["onesf"][:], d["onesf_d"], q="qs")
    d["eps"] = d["cc"][:, 0:1]
    d["eighth"] = d["cc"][:, 1:2]
    d["dequant"] = d["cc"][:, 3:4]
    return d


def load_vec_cols(p, name, vec_dram, n, q="qs"):
    t = p.sb(name, (128, n), F32)
    for i in range(n):
        src = vec_dram[i * 128:(i + 1) * 128].rearrange("(p o) -> p o", o=1)
        p.dma(t[:, i:i + 1], src, q=q)
    return t


class LnScr:
    def __init__(self, p, tag, n=4):
        self.n = n
        self.bst = [p.sb(f"{tag}_bst{i}", (128, 12), F32) for i in range(n)]
        self.agg = [p.sb(f"{tag}_agg{i}", (128, 2), F32) for i in range(n)]
        self.rstd = [p.sb(f"{tag}_rstd{i}", (128, 1), F32) for i in range(n)]
        self.nmu = [p.sb(f"{tag}_nmu{i}", (128, 1), F32) for i in range(n)]
        self.i = 0

    def next(self):
        i = self.i % self.n
        self.i += 1
        return self.bst[i], self.agg[i], self.rstd[i], self.nmu[i]


def ln_stats(p, C, scr, src_tile, ncols=1024):
    bst, agg, rstd, nmu = scr.next()
    nch = (ncols + 511) // 512
    for c in range(nch):
        p.bnstats(bst[:, 6 * c:6 * c + 6], src_tile[:, 512 * c:min(512 * (c + 1), ncols)])
    p.bnaggr(agg[:], bst[:, 0:6 * nch])
    p.act(nmu[:], agg[:, 1:2], AF.Ln, bias=C["eps"])
    p.act(rstd[:], nmu[:], AF.Exp, scale=-0.5)
    p.stt(nmu[:], agg[:, 0:1], -1.0, rstd[:], MULT, MULT)
    return rstd, nmu


def _ap(x):
    return x if isinstance(x, bass.AP) else x[:]


def epilogue_tile(p, C, scr, blk_tile, b, ti, x_dram, xo_dram, xbuf, lnbuf):
    xbuf = _ap(xbuf)
    lnbuf = _ap(lnbuf)
    rstd, nmu = ln_stats(p, C, scr, blk_tile)
    p.act(lnbuf, blk_tile, AF.Identity, bias=nmu[:, 0:1], scale=rstd[:, 0:1])
    p.dma(xbuf, x_dram[b, ti * 128:(ti + 1) * 128, :], q="qs")
    p.tt(xbuf, xbuf, lnbuf, ADD)
    p.dma(xo_dram[b, ti * 128:(ti + 1) * 128, :], xbuf, q="qg")



def declare_block0(p):
    return {
        "w_qkv": p.dram("w_qkv", [1024, 3072], F32, kind="ExternalInput"),
        "b_qkv": p.dram("b_qkv", [3072], F32, kind="ExternalInput"),
        "w_proj": p.dram("w_proj", [1024, 1024], F32, kind="ExternalInput"),
        "b_proj": p.dram("b_proj", [1024], F32, kind="ExternalInput"),
    }


def block0_body(p, C, scr, pf, pb, ins, x_d, xo_d, nbatch=2):
    wqkv_d = ins["w_qkv"]
    bqkv_d = ins["b_qkv"]
    wproj_d = ins["w_proj"]
    bproj_d = ins["b_proj"]

    bq = load_vec_cols(p, "bq", bqkv_d, 24)
    bp = load_vec_cols(p, "bp", bproj_d, 8)

    xnT = p.sb("xnT", (128, 16384), BF16)
    qT = [p.sb(f"qT{j}", (128, 2048), BF16) for j in range(8)]
    kT = [p.sb(f"kT{j}", (128, 2048), BF16) for j in range(8)]
    vtok = p.sb("vtok", (128, 16 * 1040), BF16)
    attnT = xnT
    eS = [p.sb(f"eS{h}", (128, 8 * 512), BF16) for h in range(4)]

    xt = [p.sb(f"xt{i}", (128, 1024), F32) for i in range(2)]
    xnt = [p.sb(f"xnt{i}", (128, 1024), BF16) for i in range(2)]
    xtA = xt + [eS[3][:].bitcast(F32)[:, 0:1024]]
    xntA = xnt + [eS[3][:, 2048:3072]]
    wst = [p.sb(f"wst{i}", (128, 128), F32) for i in range(4)]
    wqb = [p.sb(f"wqb{i}", (128, 128), BF16) for i in range(16)]
    wpb = [p.sb(f"wpb{i}", (128, 1024), BF16) for i in range(8)]

    pjt = [p.sb(f"pjt{i}", (128, 512), BF16) for i in range(2)]
    otokd = [eS[2], eS[3]]
    lnb = [eS[0][:].bitcast(F32)[:, 0:1024], eS[1][:].bitcast(F32)[:, 0:1024]]
    xrb = xt
    avs = [p.sb(f"avs{i}", (64, 512), BF16) for i in range(2)]
    wpf = xt[0][:]
    rden = lnb[0][0:1, 0:512]
    rrow = lnb[1][0:1, 0:512]

    for b in range(nbatch):
        prologue_to_xnT(p, C, scr, x_d, b, xnT, xtA, xntA, pb)

        for ft in range(24):
            wo = 8 * (ft % 2)
            for dt in range(8):
                ws = wst[dt % 4]
                p.dma(ws[:], wqkv_d[dt * 128:(dt + 1) * 128, ft * 128:(ft + 1) * 128], q="qs")
                p.copy(wqb[wo + dt][:], ws[:], eng="v")
            for tc in range(4):
                psq = pf[tc]
                for dt in range(8):
                    p.mm(psq, wqb[wo + dt][:], xnT[:, dt * 2048 + tc * 512: dt * 2048 + (tc + 1) * 512],
                         start=(dt == 0), stop=(dt == 7), commit=(dt == 7))
                if ft < 16:
                    if tc % 2 == 1:
                        dstT = qT[ft] if ft < 8 else kT[ft - 8]
                        p.evac(dstT[:, (tc - 1) * 512:(tc + 1) * 512],
                               p.pw[tc // 2][:], bias=bq[:, ft:ft + 1], eng="v")
                else:
                    vs = pjt[tc % 2]
                    p.evac(vs[:], psq, bias=bq[:, ft:ft + 1], eng="v")
                    hj = ft - 16
                    pbt = pb[tc % 2]
                    for q4 in range(4):
                        p.trans(pbt[:, q4 * 128:(q4 + 1) * 128],
                                vs[:, q4 * 128:(q4 + 1) * 128], C["idn"][:],
                                commit=(q4 == 3))
                    base = tc * 4 * 1040 + (2 * hj) * 65
                    dst = bass.AP(vtok, base, [[16640, 128], [1040, 4], [65, 2], [1, 64]])
                    srcv = pbt[:, 0:512].rearrange("p (q h f) -> p q h f", q=4, h=2)
                    p.copy(dst, srcv, eng="v")
        if b == 0:
            for kt in range(16):
                for h in range(16):
                    p.memset(vtok[:, kt * 1040 + h * 65 + 64: kt * 1040 + h * 65 + 65], 1.0, eng="g")

        for hp in range(8):
            for qc in range(4):
                for kh in range(2):
                    for k8 in range(8):
                        kt = kh * 8 + k8
                        ps1 = pf[k8 % 2]
                        ps2 = pf[2 + k8 % 2]
                        p.mm(ps1, kT[hp][0:64, kt * 128:(kt + 1) * 128],
                             qT[hp][0:64, qc * 512:(qc + 1) * 512], tp=(0, 0))
                        p.mm(ps2, kT[hp][64:128, kt * 128:(kt + 1) * 128],
                             qT[hp][64:128, qc * 512:(qc + 1) * 512], tp=(64, 0))
                        if k8 % 2 == 1:
                            p.act(eS[0 + 2 * kh][:, (k8 - 1) * 512:(k8 + 1) * 512],
                                  p.pw[0][:], AF.Exp, scale=C["eighth"])
                            p.act(eS[1 + 2 * kh][:, (k8 - 1) * 512:(k8 + 1) * 512],
                                  p.pw[1][:], AF.Exp, scale=C["eighth"])
                for kh in range(2):
                    for h01 in range(2):
                        h = hp * 2 + h01
                        psav = pf[4 + h01]
                        for k8 in range(8):
                            kt = kh * 8 + k8
                            p.mm(psav[0:65, :], vtok[:, kt * 1040 + h * 65: kt * 1040 + h * 65 + 65],
                                 eS[h01 + 2 * kh][:, k8 * 512:(k8 + 1) * 512],
                                 start=(kt == 0), stop=(kt == 15), commit=(k8 == 7))
                for h01 in range(2):
                    psav = pf[4 + h01]
                    p.act(rden, psav[64:65, :], AF.Ln)
                    p.act(rrow, rden, AF.Exp, scale=-1.0)
                    psden = pb[h01][:].bitcast(F32)
                    p.mm(psden[0:64, :], C["onesf"][0:1, 0:64], rrow)
                    av = avs[h01]
                    p.copy(av[:], psav[0:64, :], eng="v")
                    p.tt(attnT[64 * h01: 64 * h01 + 64, hp * 2048 + qc * 512: hp * 2048 + (qc + 1) * 512],
                         av[:], psden[0:64, :], MULT)

        for dt in range(8):
            p.dma(wpf, wproj_d[dt * 128:(dt + 1) * 128, :], q="qs")
            p.copy(wpb[dt][:], wpf, eng="v")
        for tc in range(4):
            for ot in range(8):
                psp = pf[ot % 4]
                for dt in range(8):
                    p.mm(psp, wpb[dt][:, ot * 128:(ot + 1) * 128],
                         attnT[:, dt * 2048 + tc * 512: dt * 2048 + (tc + 1) * 512],
                         start=(dt == 0), stop=(dt == 7), commit=(dt == 7))
                pj = pjt[ot % 2]
                p.evac(pj[:], psp, bias=bp[:, ot:ot + 1], eng="v")
                batched_out_tile(p, C, pb, ot, pj[:], otokd[0], ot, eng="v")
            for q4 in range(4):
                ti = tc * 4 + q4
                epilogue_tile(p, C, scr, otokd[0][:, q4 * 1024:(q4 + 1) * 1024], b, ti,
                              x_d, xo_d, xrb[q4 % 2], lnb[q4 % 2])


def common_pools(p):
    C = decl_consts(p)
    scr = LnScr(p, "ln")
    pw = [p.ps(f"pw{i}", (128, 1024), F32) for i in range(2)]
    pnx = p.ps("pnx", (128, 1024), F32)
    pb = [p.ps(f"pb{i}", (128, 1024), BF16) for i in range(2)]
    pf = [pw[0][:, 0:512], pw[0][:, 512:1024], pw[1][:, 0:512], pw[1][:, 512:1024],
          pnx[:, 0:512], pnx[:, 512:1024]]
    p.pw = pw
    p.pnx = pnx
    return C, scr, pf, pb


def build_block0(p, nbatch=2):
    x_d = p.dram("x", [nbatch, 2048, 1024], F32, kind="ExternalInput")
    xo_d = p.dram("xo", [nbatch, 2048, 1024], F32, kind="ExternalOutput")
    ins = declare_block0(p)
    C, scr, pf, pb = common_pools(p)
    p.push()
    block0_body(p, C, scr, pf, pb, ins, x_d, xo_d, nbatch)
    p.pop()
    p.drain()
    return p


def prologue_ti(p, C, scr, x_d, b, xnT, xt, xnt, pb, ti):
    xa = xt[ti % len(xt)]
    xna = xnt[ti % len(xnt)]
    xa_, xna_ = _ap(xa), _ap(xna)
    p.dma(xa_, x_d[b, ti * 128:(ti + 1) * 128, :], q="qs")
    rstd, nmu = ln_stats(p, C, scr, xa_)
    p.act(xna_, xa_, AF.Identity, bias=nmu[:, 0:1], scale=rstd[:, 0:1])
    pbt = pb[ti % 2]
    for j in range(8):
        p.trans(pbt[:, j * 128:(j + 1) * 128], xna[:, j * 128:(j + 1) * 128],
                C["idn"][:], commit=(j == 7))
    dst = bass.AP(xnT.tensor if isinstance(xnT, bass.AP) else xnT,
                  ti * 128, [[16384, 128], [2048, 8], [1, 128]])
    p.copy(dst, pbt[:].rearrange("p (j f) -> p j f", j=8), eng="v")


def prologue_to_xnT(p, C, scr, x_d, b, xnT, xt, xnt, pb):
    for ti in range(16):
        prologue_ti(p, C, scr, x_d, b, xnT, xt, xnt, pb, ti)


def batched_out_tile(p, C, pb, pbi, src_bf16_512, otok, ot, eng="s"):
    pbt = pb[pbi % 2]
    for q4 in range(4):
        p.trans(pbt[:, q4 * 128:(q4 + 1) * 128],
                src_bf16_512[:, q4 * 128:(q4 + 1) * 128], C["idn"][:],
                commit=(q4 == 3))
    dst = bass.AP(otok, ot * 128, [[4096, 128], [1024, 4], [1, 128]])
    p.copy(dst, pbt[:, 0:512].rearrange("p (q f) -> p q f", q=4), eng=eng)


def out_transpose_epilogue(p, C, scr, b, tc, evac_src_psum_fn, nout, x_d, xo_d,
                           pb, pjt, otok, lnb, xrb, pre_add_fn=None):
    for ot in range(nout):
        ps, bias, scale = evac_src_psum_fn(ot)
        pj = pjt[ot % 2]
        kw = {}
        if bias is not None:
            kw["bias"] = bias
        if scale is not None:
            kw["scale"] = scale
        p.act(pj[:], ps, AF.Identity, **kw)
        for q4 in range(4):
            pbt = pb[q4 % 2]
            p.trans(pbt[:, 0:128], pj[:, q4 * 128:(q4 + 1) * 128], C["idn"][:])
            p.copy(otok[q4][:, ot * 128:(ot + 1) * 128], pbt[:, 0:128], eng="s")
    for q4 in range(4):
        ti = tc * 4 + q4
        src = otok[q4][:]
        if pre_add_fn is not None:
            src = pre_add_fn(q4, ti)
        epilogue_tile(p, C, scr, src, b, ti, x_d, xo_d, xrb[q4 % 2], lnb[q4 % 2])



def declare_block1(p):
    return {
        "w_gate": p.dram("w_gate", [3072, 1024], I32, kind="ExternalInput"),
        "w_up": p.dram("w_up", [3072, 1024], I32, kind="ExternalInput"),
        "w_down": p.dram("w_down", [1024, 3072], I32, kind="ExternalInput"),
    }


def block1_body(p, C, scr, pf, pb, ins, x_d, xo_d, nbatch=2):
    wg_d = ins["w_gate"]
    wu_d = ins["w_up"]
    wd_d = ins["w_down"]

    xnT = p.sb("b1xnT", (128, 16384), BF16)
    hb = [p.sb(f"hb{j}", (128, 2048), BF16) for j in range(24)]
    xnt = [hb[23][:, 0:1024], hb[23][:, 1024:2048]]

    xt = [p.sb(f"xt{i}", (128, 1024), F32) for i in range(2)]
    wif = [p.sb(f"wif{i}", (128, 1024), I32) for i in range(2)]
    wib = [p.sb(f"wib{i}", (128, 1024), BF16) for i in range(2)]
    wgT = [p.sb(f"wgT{i}", (128, 128), BF16) for i in range(16)]
    wuT = [p.sb(f"wuT{i}", (128, 128), BF16) for i in range(16)]
    wdT = [p.sb(f"wdT{i}", (128, 128), BF16) for i in range(24)]
    sg1 = p.sb("sg1", (128, 1024), BF16)
    sdT = [p.sb(f"sdT{i}", (128, 2048), BF16) for i in range(8)]
    otok = p.sb("b1otok", (128, 4096), BF16)
    lnb = [xnT[:].bitcast(F32)[:, 0:1024], xnT[:].bitcast(F32)[:, 1024:2048]]
    xrb = xt

    s2col = p.sb("s2col", (128, 1), F32)
    p.act(s2col[:], C["dequant"], AF.Square)

    for b in range(nbatch):
        prologue_to_xnT(p, C, scr, x_d, b, xnT, xt, xnt, pb)

        for ht in range(24):
            wo = 8 * (ht % 2)
            for wk, (w_d, wT) in enumerate(((wg_d, wgT), (wu_d, wuT))):
                wi = wif[wk]
                wb = wib[wk]
                p.dma(wi[:], w_d[ht * 128:(ht + 1) * 128, :], q="qs")
                p.copy(wb[:], wi[:], eng="v")
                for j in range(8):
                    pbt = pb[j % 2]
                    p.trans(pbt[:, 0:128], wb[:, j * 128:(j + 1) * 128], C["idn"][:])
                    p.copy(wT[wo + j][:], pbt[:, 0:128], eng="s")
            for tc in range(4):
                psg = pf[0 + tc % 2]
                psu = pf[2 + tc % 2]
                for dtt in range(8):
                    p.mm(psg, wgT[wo + dtt][:], xnT[:, dtt * 2048 + tc * 512: dtt * 2048 + (tc + 1) * 512],
                         start=(dtt == 0), stop=(dtt == 7), commit=(dtt == 7))
                for dtt in range(8):
                    p.mm(psu, wuT[wo + dtt][:], xnT[:, dtt * 2048 + tc * 512: dtt * 2048 + (tc + 1) * 512],
                         start=(dtt == 0), stop=(dtt == 7), commit=(dtt == 7))
                if tc % 2 == 1:
                    p.act(sg1[:], p.pw[0][:], AF.Silu, scale=C["dequant"])
                    p.tt(hb[ht][:, (tc - 1) * 512:(tc + 1) * 512], sg1[:], p.pw[1][:], MULT)

        for dot in range(8):
            for c3 in range(3):
                wi = wif[c3 % 2]
                wb = wib[c3 % 2]
                p.dma(wi[:], wd_d[dot * 128:(dot + 1) * 128, c3 * 1024:(c3 + 1) * 1024], q="qs")
                p.copy(wb[:], wi[:], eng="v")
                for h8 in range(8):
                    ht = c3 * 8 + h8
                    pbt = pb[ht % 2]
                    p.trans(pbt[:, 0:128], wb[:, h8 * 128:(h8 + 1) * 128], C["idn"][:])
                    p.copy(wdT[ht][:], pbt[:, 0:128], eng="s")
            for tc in range(4):
                psd = pf[4 + tc % 2]
                for ht in range(24):
                    p.mm(psd, wdT[ht][:], hb[ht][:, tc * 512:(tc + 1) * 512],
                         start=(ht == 0), stop=(ht == 23), commit=(ht == 23))
                p.evac(sdT[dot][:, tc * 512:(tc + 1) * 512], psd,
                       scale=s2col[:, 0:1], eng="v")
        for tc in range(4):
            for ot in range(8):
                batched_out_tile(p, C, pb, ot, sdT[ot][:, tc * 512:(tc + 1) * 512],
                                 otok, ot, eng="v")
            for q4 in range(4):
                ti = tc * 4 + q4
                epilogue_tile(p, C, scr, otok[:, q4 * 1024:(q4 + 1) * 1024], b, ti,
                              x_d, xo_d, xrb[q4 % 2], lnb[q4 % 2])


def build_block1(p, nbatch=2):
    x_d = p.dram("x", [nbatch, 2048, 1024], F32, kind="ExternalInput")
    xo_d = p.dram("xo", [nbatch, 2048, 1024], F32, kind="ExternalOutput")
    ins = declare_block1(p)
    C, scr, pf, pb = common_pools(p)
    p.push()
    block1_body(p, C, scr, pf, pb, ins, x_d, xo_d, nbatch)
    p.pop()
    p.drain()
    return p



def declare_block3(p):
    return {
        "weight": p.dram("weight", [1024, 1024], F32, kind="ExternalInput"),
        "disha": p.dram("disha", [64, 1024], F32, kind="ExternalInput"),
    }


def block3_body(p, C, scr, pf, pb, ins, x_d, xo_d, nbatch=2):
    w_d = ins["weight"]
    dsh_d = ins["disha"]

    xnT = p.sb("b3xnT", (128, 16384), BF16)
    xnT2 = p.sb("b3xnT2", (128, 16384), BF16)
    xt = [p.sb(f"xt{i}", (128, 1024), F32) for i in range(2)]
    xnt = [p.sb(f"xnt{i}", (128, 1024), BF16) for i in range(2)]
    xt2 = [p.sb(f"x2t{i}", (128, 1024), F32) for i in range(2)]
    xnt2 = [p.sb(f"x2nt{i}", (128, 1024), BF16) for i in range(2)]
    scr2 = LnScr(p, "ln3b")
    wf = p.sb("wf", (128, 1024), F32)
    wb = p.sb("wb", (128, 1024), BF16)
    wT = [p.sb(f"wT{i}", (128, 1024), BF16) for i in range(8)]
    dsf = p.sb("dsf", (64, 1024), F32)
    dsb = p.sb("dsb", (64, 1024), BF16)
    poolb = p.sb("poolb", (64, 512), BF16)
    sdT = [p.sb(f"sdT{i}", (128, 2048), BF16) for i in range(8)]
    otok = p.sb("b3otok", (128, 4096), BF16)
    lnb = [p.sb(f"lnb{i}", (128, 1024), F32) for i in range(1)]
    xrb = xt

    for ot in range(8):
        p.dma(wf[:], w_d[ot * 128:(ot + 1) * 128, :], q="qs")
        p.copy(wb[:], wf[:], eng="v")
        for dt in range(8):
            pbt = pb[dt % 2]
            p.trans(pbt[:, 0:128], wb[:, dt * 128:(dt + 1) * 128], C["idn"][:])
            p.copy(wT[dt][:, ot * 128:(ot + 1) * 128], pbt[:, 0:128], eng="s")
    p.dma(dsf[:], dsh_d, q="qs")
    p.copy(dsb[:], dsf[:], eng="v")

    xnTs = [xnT, xnT2]
    prologue_to_xnT(p, C, scr, x_d, 0, xnT, xt, xnt, pb)
    for b in range(nbatch):
        xnTb = xnTs[b % 2]
        for tc in range(4):
            psp = pf[4]
            for dt in range(8):
                p.mm(psp[0:64, :], C["eye64"][:],
                     xnTb[:, dt * 2048 + tc * 512: dt * 2048 + (tc + 1) * 512],
                     start=(dt == 0), stop=(dt == 7), commit=(dt == 7))
            p.act(poolb[:], psp[0:64, :], AF.Identity)
            for ot in range(8):
                psr = pf[ot % 4]
                for dt in range(8):
                    p.mm(psr, wT[dt][:, ot * 128:(ot + 1) * 128],
                         xnTb[:, dt * 2048 + tc * 512: dt * 2048 + (tc + 1) * 512],
                         start=(dt == 0), stop=False, commit=False)
                p.mm(psr, dsb[0:64, ot * 128:(ot + 1) * 128], poolb[:],
                     start=False, stop=True, commit=True)
                p.act(sdT[ot][:, tc * 512:(tc + 1) * 512], psr, AF.Identity)
            for ot in range(8):
                batched_out_tile(p, C, pb, ot, sdT[ot][:, tc * 512:(tc + 1) * 512],
                                 otok, ot)
            for q4 in range(4):
                ti = tc * 4 + q4
                epilogue_tile(p, C, scr, otok[:, q4 * 1024:(q4 + 1) * 1024], b, ti,
                              x_d, xo_d, xrb[q4 % 2], lnb[0])
                if b + 1 < nbatch:
                    prologue_ti(p, C, scr2, x_d, b + 1, xnTs[(b + 1) % 2],
                                xt2, xnt2, pb, ti)


def build_block3(p, nbatch=2):
    x_d = p.dram("x", [nbatch, 2048, 1024], F32, kind="ExternalInput")
    xo_d = p.dram("xo", [nbatch, 2048, 1024], F32, kind="ExternalOutput")
    ins = declare_block3(p)
    C, scr, pf, pb = common_pools(p)
    p.push()
    block3_body(p, C, scr, pf, pb, ins, x_d, xo_d, nbatch)
    p.pop()
    p.drain()
    return p



def declare_block2(p, nbatch=2):
    return {
        "ln_g": p.dram("ln_g", [1024], F32, kind="ExternalInput"),
        "ln_b": p.dram("ln_b", [1024], F32, kind="ExternalInput"),
        "W_in": p.dram("W_in", [1024, 4096], F32, kind="ExternalInput"),
        "conv_w": p.dram("conv_w", [2048, 4], F32, kind="ExternalInput"),
        "conv_b": p.dram("conv_b", [2048], F32, kind="ExternalInput"),
        "W_x": p.dram("W_x", [2048, 96], F32, kind="ExternalInput"),
        "W_dt": p.dram("W_dt", [64, 2048], F32, kind="ExternalInput"),
        "b_dt": p.dram("b_dt", [2048], F32, kind="ExternalInput"),
        "A_log": p.dram("A_log", [2048, 16], F32, kind="ExternalInput"),
        "D": p.dram("D", [2048], F32, kind="ExternalInput"),
        "W_out": p.dram("W_out", [2048, 1024], F32, kind="ExternalInput"),
        "xin_s": p.dram("xin_s", [nbatch, 2048, 1024], BF16),
        "dt_sc": p.dram("dt_sc", [nbatch, 16, 128, 2048], BF16),
        "z_s": p.dram("z_s", [nbatch, 16, 128, 2048], BF16),
        "y_s": p.dram("y_s", [nbatch, 16, 128, 4096], BF16),
        "xh_s": p.dram("xh_s", [nbatch, 16, 128, 2048], BF16),
        "xc_s": p.dram("xc_s", [nbatch, 16, 128, 2048], BF16),
        "bc_s": p.dram("bc_s", [32, 2048], BF16),
    }


def block2_params(p, ins):
    P2 = {}
    P2["lg"] = load_vec_cols(p, "lg", ins["ln_g"], 8, q="qg")
    P2["lb"] = load_vec_cols(p, "lb", ins["ln_b"], 8, q="qg")
    P2["cb"] = load_vec_cols(p, "cbv", ins["conv_b"], 16, q="qg")
    P2["bdt"] = load_vec_cols(p, "bdtv", ins["b_dt"], 16, q="qg")
    P2["dcol"] = load_vec_cols(p, "dcol", ins["D"], 16, q="qg")

    cw = p.sb("cw", (128, 64), F32)
    A_sb = p.sb("A_sb", (128, 256), F32)
    astg = p.sb("astg", (128, 16), F32)
    for dt in range(16):
        p.dma(cw[:, dt * 4:(dt + 1) * 4], ins["conv_w"][dt * 128:(dt + 1) * 128, :], q="qg")
        p.dma(astg[:], ins["A_log"][dt * 128:(dt + 1) * 128, :], q="qg")
        p.act(A_sb[:, dt * 16:(dt + 1) * 16], astg[:], AF.Exp)
        p.ts(A_sb[:, dt * 16:(dt + 1) * 16], A_sb[:, dt * 16:(dt + 1) * 16], -1.0, MULT)
    P2["cw"] = cw
    P2["A_sb"] = A_sb
    return P2


def block2_body(p, C, scr, pf, pb, ins, x_d, xo_d, nbatch=2, P2=None):
    win_d = ins["W_in"]
    wout_d = ins["W_out"]
    xin_s = ins["xin_s"]; z_s = ins["z_s"]; y_s = ins["y_s"]; dt_sc = ins["dt_sc"]
    xh_s = ins["xh_s"]; xc_s = ins["xc_s"]

    if P2 is None:
        P2 = block2_params(p, ins)
    lg = P2["lg"]; lb = P2["lb"]; cb = P2["cb"]; bdt = P2["bdt"]
    dcol = P2["dcol"]; cw = P2["cw"]; A_sb = P2["A_sb"]

    Wx_sb = p.sb("Wx_sb", (128, 16 * 96), BF16)
    wxf = p.sb("wxf", (128, 96), F32)
    for dt in range(16):
        p.dma(wxf[:], ins["W_x"][dt * 128:(dt + 1) * 128, :], q="qs")
        p.copy(Wx_sb[:, dt * 96:(dt + 1) * 96], wxf[:], eng="v")

    wdtstg = p.sb("wdtstg", (64, 1024), F32)
    Wdt_sb = p.sb("Wdt_sb", (64, 2048), BF16)
    for h in range(2):
        p.dma(wdtstg[:], ins["W_dt"][:, h * 1024:(h + 1) * 1024], q="qs")
        p.copy(Wdt_sb[:, h * 1024:(h + 1) * 1024], wdtstg[:], eng="v")

    bfB = [p.sb(f"bfB{i}", (128, 2048), BF16) for i in range(8)]
    bfC = [p.sb(f"bfC{i}", (128, 2048), BF16) for i in range(8)]
    w8 = [p.sb(f"w8_{i}", (128, 2048), BF16) for i in range(8)]
    xcs = [p.sb(f"xcs{i}", (128, 2052), BF16) for i in range(2)]
    fA = [p.sb(f"fA{i}", (128, 2048), F32) for i in range(3)]
    yq = [p.sb(f"yq{i}", (128, 2048), BF16) for i in range(2)]
    fB = [p.sb(f"fB{i}", (128, 2048), BF16) for i in range(6)]
    at3 = p.sb("at3", (128, 2048), BF16)
    hcb = [fB[5], fB[5]]
    du2 = [fB[0], p.sb("du1", (128, 2048), BF16)]
    dbc = p.sb("dbc", (96, 2048), BF16)
    brow = [wdtstg[0:1, :].bitcast(BF16)]
    wpl = bfB + w8

    xt = [fA[0][:, 0:1024], fA[1][:, 0:1024], fA[2][:, 0:1024], fA[2][:, 1024:2048]]
    xnt = [fB[1][:, 0:1024], fB[1][:, 1024:2048], fB[2][:, 0:1024], fB[2][:, 1024:2048]]
    xint = [p.sb(f"xint{i}", (128, 1024), BF16) for i in range(2)]
    xint3 = xint + [at3[:, 0:1024], at3[:, 1024:2048]]
    wst = [fB[5][:].bitcast(F32)[:, 128 * i:128 * (i + 1)] for i in range(4)]
    wsb = [fB[2][:, 128 * i:128 * (i + 1)] for i in range(8)] + [fB[3][:, 128 * i:128 * (i + 1)] for i in range(8)]
    zst = [fB[0][:, 0:512], fB[0][:, 512:1024]]
    pjt = [fB[0][:, 1024:1536], fB[0][:, 1536:2048]]

    for b in range(nbatch):
        for ti in range(16):
            xa = xt[ti % 4]
            xna = xnt[ti % 4]
            xia = xint3[ti % 4]
            p.dma(xa, x_d[b, ti * 128:(ti + 1) * 128, :], q="qs")
            rstd, nmu = ln_stats(p, C, scr, xa)
            p.act(xia[:], xa, AF.Identity, bias=nmu[:, 0:1], scale=rstd[:, 0:1])
            p.dma(xin_s[b, ti * 128:(ti + 1) * 128, :], xia[:], q="qg")
            rstd, nmu = ln_stats(p, C, scr, xia[:])
            p.act(xna, xia[:], AF.Identity, bias=nmu[:, 0:1], scale=rstd[:, 0:1])
            for j in range(8):
                pbt = pb[j % 2]
                p.trans(pbt[:, 0:128], xna[:, j * 128:(j + 1) * 128], C["idn"][:])
                p.act(bfB[j][:, ti * 128:(ti + 1) * 128], pbt[:, 0:128], AF.Identity,
                      scale=lg[:, j:j + 1], bias=lb[:, j:j + 1])

        for ot in range(32):
            wo = 8 * (ot % 2)
            for dt8 in range(8):
                ws = wst[dt8 % 4]
                p.dma(ws, win_d[dt8 * 128:(dt8 + 1) * 128, ot * 128:(ot + 1) * 128], q="qs")
                p.copy(wsb[wo + dt8], ws, eng="v")
            for tc in range(4):
                ps = pf[tc]
                for dt8 in range(8):
                    p.mm(ps, wsb[wo + dt8], bfB[dt8][:, tc * 512:(tc + 1) * 512],
                         start=(dt8 == 0), stop=(dt8 == 7), commit=(dt8 == 7))
                if tc % 2 == 1:
                    if ot < 16:
                        xq = yq[ot % 2]
                        p.act(xq[:, (tc - 1) * 512:(tc + 1) * 512],
                              p.pw[tc // 2][:], AF.Identity)
                        if tc == 3:
                            p.dma(xh_s[b, ot], xq[:], q="qg")
                    else:
                        zt = fB[0][:, 0:1024] if tc == 1 else fB[0][:, 1024:2048]
                        p.act(zt, p.pw[tc // 2][:], AF.Identity)
                        p.dma(z_s[b, ot - 16, :, (tc - 1) * 512:(tc + 1) * 512], zt, q="qg")

        psdc = [pf[4], pf[5], pb[0][:].bitcast(F32), pb[1][:].bitcast(F32)]
        for dt in range(16):
            xc = xcs[dt % 2]
            p.dma(xc[:, 3:2051], xh_s[b, dt], q="qs")
            c0 = fA[0] if dt % 2 == 0 else fA[2]
            c1 = fA[1] if dt % 2 == 0 else fA[0]
            p.memset(xc[:, 0:3], 0, eng="g")
            p.ts(c0[:], xc[:, 3:2051], cw[:, dt * 4 + 3: dt * 4 + 4], MULT,
                 s2=cb[:, dt:dt + 1], op1=ADD)
            p.stt(c1[:], xc[:, 2:2050], cw[:, dt * 4 + 2: dt * 4 + 3], c0[:], MULT, ADD)
            p.stt(c0[:], xc[:, 1:2049], cw[:, dt * 4 + 1: dt * 4 + 2], c1[:], MULT, ADD)
            p.stt(c1[:], xc[:, 0:2048], cw[:, dt * 4 + 0: dt * 4 + 1], c0[:], MULT, ADD)
            p.act(xc[:, 3:2051], c1[:], AF.Silu)
            p.dma(xc_s[b, dt], xc[:, 3:2051], q="qg")
            for c in range(4):
                p.mm(psdc[c][0:96, :], Wx_sb[:, dt * 96:(dt + 1) * 96],
                     xc[:, 3 + c * 512: 3 + (c + 1) * 512],
                     start=(dt == 0), stop=(dt == 15), commit=(c == 3))
        for c in range(4):
            p.copy(dbc[64:96, c * 512:(c + 1) * 512], psdc[c][64:96, :], eng="s")
        p.dma(ins["bc_s"], dbc[64:96, :], q="qg")
        for c in range(4):
            p.copy(dbc[0:64, c * 512:(c + 1) * 512], psdc[c][0:64, :], eng="s")

        yacc = [pf[4], pf[5], pb[0][:].bitcast(F32), pb[1][:].bitcast(F32)]
        dtb = [fA[0][:].bitcast(BF16)[:, 0:2048], fA[0][:].bitcast(BF16)[:, 2048:4096]]
        aTb = [fA[2][:].bitcast(BF16)[:, 0:2048], fA[2][:].bitcast(BF16)[:, 2048:4096],
               at3[:]]
        def bcast_bc(nh, n8, which):
            roff, dest, q = (0, bfB, "qs") if which == 0 else (16, bfC, "qg")
            src = bass.AP(ins["bc_s"].tensor, (roff + nh * 8 + n8) * 2048,
                          [[0, 128], [1, 2048]])
            p.dma(dest[n8][:], src, q=q)

        for nh in range(2):
            if nh == 0:
                for n8 in range(8):
                    bcast_bc(0, n8, 0)
                    bcast_bc(0, n8, 1)
            if nh == 0:
                p.dma(xcs[0][:, 3:2051], xc_s[b, 0], q="qs")

            def emit_head(dtn, stage):
                if dtn >= 16:
                    return
                dtcn = dtb[dtn % 2]
                if nh == 0:
                    if stage in (0, 1, 2, 3) and stage % 2 == 0:
                        for c in (stage, stage + 1):
                            psq = pf[2 + c % 2]
                            p.mm(psq[:], Wdt_sb[0:64, dtn * 128:(dtn + 1) * 128],
                                 dbc[0:64, c * 512:(c + 1) * 512])
                    elif stage in (1, 3):
                        for c in (stage - 1, stage):
                            psq = pf[2 + c % 2]
                            p.act(fA[1][:, c * 512:(c + 1) * 512], psq[:], AF.Exp,
                                  bias=bdt[:, dtn:dtn + 1])
                    elif stage == 4:
                        p.act(dtcn, fA[1][:], AF.Ln, bias=C["cc"][:, 2:3])
                        p.dma(dt_sc[b, dtn], dtcn, q="qg")
                    else:
                        p.tt(du2[dtn % 2][:], dtcn, xcs[dtn % 2][:, 3:2051],
                             MULT, eng="g")
                else:
                    if stage == 0:
                        p.dma(dtcn, dt_sc[b, dtn], q="qs")
                        p.dma(yq[dtn % 2][:], y_s[b, dtn, :, 0:2048], q="qs")
                    elif stage == 5:
                        p.tt(du2[dtn % 2][:], dtcn, xcs[dtn % 2][:, 3:2051],
                             MULT, eng="g")

            for st in range(6):
                if nh == 1 and st == 0:
                    continue
                emit_head(0, st)
            tail_prev = None
            for dt in range(16):
                dtc = dtb[dt % 2]
                xc = xcs[dt % 2]
                du = du2[dt % 2]

                def hc_emit(n8):
                    hT_ = fB[3 + n8 % 2]
                    hc = hcb[n8 % 2]
                    p.tt(hc[:], hT_[:], bfC[n8][:], MULT, eng="v")
                    for c in range(4):
                        p.mm(yacc[c], C["idn"][:], hc[:, c * 512:(c + 1) * 512],
                             start=(n8 == 0), stop=(n8 == 7), commit=(c == 3))

                for n8 in range(8):
                    nn = nh * 8 + n8
                    aT = aTb[n8 % 3]
                    vT = fB[1 + n8 % 2]
                    hT = fB[3 + n8 % 2]
                    p.act(aT, dtc, AF.Exp, scale=A_sb[:, dt * 16 + nn: dt * 16 + nn + 1])
                    p.tt(vT[:], du[:], bfB[n8][:], MULT, eng="v")
                    p.scan(hT[:], aT, vT[:], 0.0)
                    if nh == 0 and dt == 15:
                        bcast_bc(1, n8, 0)
                        if n8 == 4:
                            p.dma(xcs[0][:, 3:2051], xc_s[b, 0], q="qs")
                            p.dma(dtb[0], dt_sc[b, 0], q="qs")
                            p.dma(yq[0][:], y_s[b, 0, :, 0:2048], q="qs")
                    if n8 == 0 and tail_prev is not None:
                        tail_prev()
                    if n8 == 1 and dt + 1 < 16:
                        p.dma(xcs[(dt + 1) % 2][:, 3:2051], xc_s[b, dt + 1], q="qs")
                    if n8 >= 1:
                        hc_emit(n8 - 1)
                        if nh == 0 and dt == 15:
                            bcast_bc(1, n8 - 1, 1)
                    if 1 <= n8 <= 4:
                        emit_head(dt + 1, n8 - 1)
                    elif n8 == 5:
                        emit_head(dt + 1, 4)
                    elif n8 == 6:
                        emit_head(dt + 1, 5)
                hc_emit(7)
                if nh == 0 and dt == 15:
                    bcast_bc(1, 7, 1)

                def make_tail(dtp, xcp, nhp):
                    def tail():
                        if nhp == 0:
                            yo = yq[dtp % 2]
                            p.act(yo[:, 0:1024], p.pnx[:, 0:1024], AF.Identity)
                            for c in (2, 3):
                                p.act(yo[:, c * 512:(c + 1) * 512], yacc[c], AF.Identity)
                            p.dma(y_s[b, dtp, :, 0:2048], yo[:], q="qg")
                        else:
                            y0 = yq[dtp % 2]
                            p.stt(fA[1][:, 0:1024], xcp[:, 3:1027],
                                  dcol[:, dtp:dtp + 1], p.pnx[:, 0:1024], MULT, ADD)
                            for c in (2, 3):
                                p.stt(fA[1][:, c * 512:(c + 1) * 512],
                                      xcp[:, 3 + c * 512: 3 + (c + 1) * 512],
                                      dcol[:, dtp:dtp + 1], yacc[c], MULT, ADD)
                            p.tt(at3[:], fA[1][:], y0[:], ADD)
                            p.dma(y_s[b, dtp, :, 2048:4096], at3[:], q="qg")
                    return tail

                tail_prev = make_tail(dt, xc, nh)
            tail_prev()
            tail_prev = None

        for dt in range(16):
            yb = fB[1 + dt % 2]
            zb = fB[3 + dt % 2]
            p.dma(yb[:], y_s[b, dt, :, 2048:4096], q="qs")
            p.dma(zb[:], z_s[b, dt], q="qs")
            p.act(hcb[dt % 2][:], zb[:], AF.Silu)
            p.tt(wpl[dt][:], yb[:], hcb[dt % 2][:], MULT)
            wos = fA[0][:, (dt % 2) * 1024:(dt % 2) * 1024 + 1024]
            p.dma(wos, wout_d[dt * 128:(dt + 1) * 128, :], q="qg")
            p.copy(bfC[dt // 2][:, (dt % 2) * 1024:(dt % 2 + 1) * 1024],
                   wos, eng="v")
        for tc in range(4):
            for ot in range(8):
                ps = pf[ot % 2]
                for dt in range(16):
                    p.mm(ps[:], bfC[dt // 2][:, (dt % 2) * 1024 + ot * 128:
                                             (dt % 2) * 1024 + (ot + 1) * 128],
                         wpl[dt][:, tc * 512:(tc + 1) * 512],
                         start=(dt == 0), stop=(dt == 15), commit=(dt == 15))
                pj = pjt[ot % 2]
                p.act(pj, ps[:], AF.Identity)
                for q4 in range(4):
                    pbt = pb[q4 % 2]
                    p.trans(pbt[:, 0:128], pj[:, q4 * 128:(q4 + 1) * 128], C["idn"][:])
                    p.copy(fB[2 + q4 // 2][:, (q4 % 2) * 1024 + ot * 128:
                                           (q4 % 2) * 1024 + (ot + 1) * 128],
                           pbt[:, 0:128], eng="s")
            for q4 in range(4):
                ti = tc * 4 + q4
                mo = fB[2 + q4 // 2][:, (q4 % 2) * 1024:(q4 % 2) * 1024 + 1024]
                xia = xint[q4 % 2]
                p.dma(xia[:], xin_s[b, ti * 128:(ti + 1) * 128, :], q="qs")
                mtile = fA[2][:, 1024 * (q4 % 2):1024 * (q4 % 2) + 1024]
                p.tt(mtile, mo, xia[:], ADD)
                epilogue_tile(p, C, scr, mtile, b, ti, x_d, xo_d,
                              fA[0][:, 1024 * (q4 % 2):1024 * (q4 % 2) + 1024],
                              yq[q4 % 2][:].bitcast(F32)[:, 0:1024])


def build_block2(p, nbatch=2):
    x_d = p.dram("x", [nbatch, 2048, 1024], F32, kind="ExternalInput")
    xo_d = p.dram("xo", [nbatch, 2048, 1024], F32, kind="ExternalOutput")
    ins = declare_block2(p, nbatch)
    C, scr, pf, pb = common_pools(p)
    p.push()
    block2_body(p, C, scr, pf, pb, ins, x_d, xo_d, nbatch)
    p.pop()
    p.drain()
    return p


def build_fused(p, nbatch=2):
    x_d = p.dram("x", [nbatch, 2048, 1024], F32, kind="ExternalInput")
    xo_d = p.dram("xo", [nbatch, 2048, 1024], F32, kind="ExternalOutput")
    x1_d = p.dram("x1s", [nbatch, 2048, 1024], F32)
    x2_d = p.dram("x2s", [nbatch, 2048, 1024], F32)
    x3_d = p.dram("x3s", [nbatch, 2048, 1024], F32)
    ins0 = declare_block0(p)
    ins1 = declare_block1(p)
    ins2 = declare_block2(p, nbatch)
    ins3 = declare_block3(p)
    C, scr, pf, pb = common_pools(p)
    P2 = block2_params(p, ins2)
    p.push(); block0_body(p, C, scr, pf, pb, ins0, x_d, x1_d, nbatch); p.pop()
    p.barrier()
    p.push(); block1_body(p, C, scr, pf, pb, ins1, x1_d, x2_d, nbatch); p.pop()
    p.barrier()
    p.push(); block2_body(p, C, scr, pf, pb, ins2, x2_d, x3_d, nbatch, P2=P2); p.pop()
    p.barrier()
    p.push(); block3_body(p, C, scr, pf, pb, ins3, x3_d, xo_d, nbatch); p.pop()
    p.drain()
    return p



N_CORES = 8


def kernel(**inputs):
    consts = host_consts()
    xs = np.ascontiguousarray(inputs["L0_attn_x"])
    shards = [xs[2 * c:2 * c + 2] for c in range(N_CORES)]

    p = P()
    build_fused(p, nbatch=2)
    with p.es:
        im = []
        for c in range(N_CORES):
            d = dict(consts)
            d.update({
                "x": shards[c],
                "w_qkv": inputs["L0_attn_w_qkv"],
                "b_qkv": inputs["L0_attn_b_qkv"],
                "w_proj": inputs["L0_attn_w_proj"],
                "b_proj": inputs["L0_attn_b_proj"],
                "w_gate": inputs["L0_ffn_w_gate"],
                "w_up": inputs["L0_ffn_w_up"],
                "w_down": inputs["L0_ffn_w_down"],
                "ln_g": inputs["L1_attn_ln_g"], "ln_b": inputs["L1_attn_ln_b"],
                "W_in": inputs["L1_attn_W_in"], "conv_w": inputs["L1_attn_conv_w"],
                "conv_b": inputs["L1_attn_conv_b"], "W_x": inputs["L1_attn_W_x"],
                "W_dt": inputs["L1_attn_W_dt"], "b_dt": inputs["L1_attn_b_dt"],
                "A_log": inputs["L1_attn_A_log"], "D": inputs["L1_attn_D"],
                "W_out": inputs["L1_attn_W_out"],
                "weight": inputs["L1_ffn_weight"],
                "disha": inputs["L1_ffn_disha"],
            })
            im.append(d)
        res = run_bass_kernel_spmd(p.nc, im, core_ids=list(range(N_CORES)))
    return np.concatenate([r["xo"] for r in res.results], axis=0).astype(np.float32)


# version 41
# speedup vs baseline: 1.0119x; 1.0119x over previous
import contextlib
import numpy as np
import ml_dtypes
import concourse.bass as bass
import concourse.mybir as mybir
from concourse.bass_utils import run_bass_kernel_spmd

F32 = mybir.dt.float32
BF16 = mybir.dt.bfloat16
I32 = mybir.dt.int32
AF = mybir.ActivationFunctionType
ALU = mybir.AluOpType
AX = mybir.AxisListType

MULT = ALU.mult
ADD = ALU.add


def fp_of(ap):
    lo = ap.offset
    hi = ap.offset
    for st, ct in ap.ap:
        d = st * (ct - 1)
        if d < 0:
            lo += d
        else:
            hi += d
    return (ap.tensor.name, lo, hi + 1)


class P:

    CENG = ("t", "v", "s", "g")
    NSUB = 16
    QENG = tuple(f"qs{i}" for i in range(16)) + tuple(f"qg{i}" for i in range(16))

    def __init__(self):
        self.nc = bass.Bass("TRN2", target_bir_lowering=False)
        self.es = contextlib.ExitStack()
        self.stacks = [self.es]
        self.sem = {}
        self.cnt = {}
        for e in self.CENG + self.QENG:
            self.sem[e] = self.es.enter_context(self.nc.semaphore("sem_" + e))
            self.cnt[e] = 0
        self.qrr = {"qs": 0, "qg": 0}
        self.waited = {e: {} for e in self.CENG + ("sy",)}
        self.rec = {}
        self.pe_pending = []
        self.n_inst = 0
        self._names = set()

    def _uniq(self, name):
        if name in self._names:
            k = 2
            while f"{name}_{k}" in self._names:
                k += 1
            name = f"{name}_{k}"
        self._names.add(name)
        return name

    def sb(self, name, shape, dtype):
        name = self._uniq(name)
        return self.stacks[-1].enter_context(self.nc.sbuf_tensor(name, list(shape), dtype))

    def ps(self, name, shape, dtype):
        return self.stacks[-1].enter_context(self.nc.psum_tensor(name, list(shape), dtype))

    def push(self):
        self.stacks.append(contextlib.ExitStack())

    def pop(self):
        self.stacks.pop().close()

    def barrier(self):
        for e in ("t", "v", "s", "g", "sy"):
            for sn in self.CENG + self.QENG:
                if self.cnt[sn] and not (e == sn):
                    self._wait(e, sn, self.cnt[sn])

    def dram(self, name, shape, dtype, kind=None):
        if kind:
            return self.nc.dram_tensor(name, list(shape), dtype, kind=kind).ap()
        return self.nc.dram_tensor(name, list(shape), dtype).ap()

    def _engine_obj(self, e):
        return {"t": self.nc.tensor, "v": self.nc.vector, "s": self.nc.scalar,
                "g": self.nc.gpsimd, "sy": self.nc.sync}[e]

    def _wait(self, eng, sname, val):
        w = self.waited[eng]
        if w.get(sname, 0) >= val:
            return
        self._engine_obj(eng).wait_ge(self.sem[sname], val)
        w[sname] = val

    def _dep(self, eng, reads, writes):
        own = {"t": "t", "v": "v", "s": "s", "g": "g", "sy": None}[eng]
        for ap in reads:
            nm, lo, hi = fp_of(ap)
            r = self.rec.get(nm)
            if not r:
                continue
            for (l2, h2, sname, val) in r["w"]:
                if l2 < hi and lo < h2:
                    if eng == "t" and sname == "t":
                        continue
                    self._wait(eng, sname, val)
        for ap in writes:
            nm, lo, hi = fp_of(ap)
            r = self.rec.get(nm)
            if not r:
                continue
            for (l2, h2, sname, val) in r["w"]:
                if l2 < hi and lo < h2:
                    if eng == "t" and sname == "t":
                        continue
                    self._wait(eng, sname, val)
            for (l2, h2, sname, val) in r["r"]:
                if l2 < hi and lo < h2:
                    if eng == "t" and sname == "t":
                        continue
                    self._wait(eng, sname, val)

    def _rec_access(self, kind, nm, lo, hi, sname, val):
        r = self.rec.setdefault(nm, {"w": [], "r": []})
        lst = r[kind]
        if kind == "w":
            r["w"] = [x for x in r["w"] if not (lo <= x[0] and x[1] <= hi)]
            r["r"] = [x for x in r["r"] if not (lo <= x[0] and x[1] <= hi)]
            lst = r["w"]
        lst.append((lo, hi, sname, val))
        if len(lst) > 96:
            del lst[0:32]

    def _commit(self, eng, inst, reads, writes, inc=True):
        self.n_inst += 1
        if eng in ("t",):
            if inc:
                self.cnt["t"] += 1
                inst.then_inc(self.sem["t"], 1)
                val = self.cnt["t"]
                for (kind, nm, lo, hi) in self.pe_pending:
                    self._rec_access(kind, nm, lo, hi, "t", val)
                self.pe_pending = []
                for ap in reads:
                    nm, lo, hi = fp_of(ap)
                    self._rec_access("r", nm, lo, hi, "t", val)
                for ap in writes:
                    nm, lo, hi = fp_of(ap)
                    self._rec_access("w", nm, lo, hi, "t", val)
            else:
                for ap in reads:
                    nm, lo, hi = fp_of(ap)
                    self.pe_pending.append(("r", nm, lo, hi))
                for ap in writes:
                    nm, lo, hi = fp_of(ap)
                    self.pe_pending.append(("w", nm, lo, hi))
        else:
            self.cnt[eng] += 1
            inst.then_inc(self.sem[eng], 1)
            val = self.cnt[eng]
            for ap in reads:
                nm, lo, hi = fp_of(ap)
                self._rec_access("r", nm, lo, hi, eng, val)
            for ap in writes:
                nm, lo, hi = fp_of(ap)
                self._rec_access("w", nm, lo, hi, eng, val)

    def mm(self, out, lhsT, rhs, start=True, stop=True, tp=None, commit=True):
        self._dep("t", [lhsT, rhs], [out])
        kw = {}
        if tp is not None:
            kw["tile_position"] = tp
        inst = self.nc.tensor.matmul(out, lhsT, rhs, start=start, stop=stop, **kw)
        self._commit("t", inst, [lhsT, rhs], [out], inc=commit)

    def trans(self, out_ps, in_sb, ident, commit=True):
        self._dep("t", [in_sb, ident], [out_ps])
        inst = self.nc.tensor.transpose(out_ps, in_sb, ident)
        self._commit("t", inst, [in_sb, ident], [out_ps], inc=commit)

    def act(self, out, in_, func, bias=0.0, scale=1.0, accum_out=None):
        reads = [in_] + [x for x in (bias, scale) if isinstance(x, bass.AP)]
        writes = [out] + ([accum_out] if accum_out is not None else [])
        self._dep("s", reads, writes)
        inst = self.nc.scalar.activation(out, in_, func, bias=bias, scale=scale,
                                         accum_out=accum_out)
        self._commit("s", inst, reads, writes)

    def tt(self, out, a, b, op, eng="v"):
        self._dep(eng, [a, b], [out])
        inst = self._engine_obj(eng).tensor_tensor(out, a, b, op)
        self._commit(eng, inst, [a, b], [out])

    def ts(self, out, in0, s1, op0, s2=None, op1=ALU.bypass, eng="v", accum_out=None):
        reads = [in0] + [x for x in (s1, s2) if isinstance(x, bass.AP)]
        writes = [out] + ([accum_out] if accum_out is not None else [])
        self._dep(eng, reads, writes)
        inst = self._engine_obj(eng).tensor_scalar(out, in0, s1, s2, op0, op1,
                                                   accum_out=accum_out)
        self._commit(eng, inst, reads, writes)

    def stt(self, out, in0, scalar, in1, op0, op1, eng="v"):
        reads = [in0, in1] + ([scalar] if isinstance(scalar, bass.AP) else [])
        self._dep(eng, reads, [out])
        inst = self._engine_obj(eng).scalar_tensor_tensor(out, in0, scalar, in1, op0, op1)
        self._commit(eng, inst, reads, [out])

    def scan(self, out, d0, d1, initial=0.0):
        reads = [d0, d1] + ([initial] if isinstance(initial, bass.AP) else [])
        self._dep("v", reads, [out])
        inst = self.nc.vector.tensor_tensor_scan(out, d0, d1, initial, MULT, ADD)
        self._commit("v", inst, reads, [out])

    def evac(self, out, psum, bias=None, scale=None, eng="s"):
        if eng == "s":
            kw = {}
            if bias is not None:
                kw["bias"] = bias
            if scale is not None:
                kw["scale"] = scale
            return self.act(out, psum, AF.Identity, **kw)
        if scale is not None and bias is not None:
            return self.ts(out, psum, scale, MULT, s2=bias, op1=ADD, eng=eng)
        if scale is not None:
            return self.ts(out, psum, scale, MULT, eng=eng)
        if bias is not None:
            return self.ts(out, psum, bias, ADD, eng=eng)
        return self.copy(out, psum, eng=eng)

    def copy(self, out, in_, eng="v"):
        if eng == "s":
            return self.act(out, in_, AF.Identity)
        self._dep(eng, [in_], [out])
        inst = self._engine_obj(eng).tensor_copy(out, in_)
        self._commit(eng, inst, [in_], [out])

    def red(self, out, in_, op, eng="v"):
        self._dep(eng, [in_], [out])
        inst = self._engine_obj(eng).tensor_reduce(out, in_, AX.X, op)
        self._commit(eng, inst, [in_], [out])

    def bnstats(self, out, in_):
        self._dep("v", [in_], [out])
        inst = self.nc.vector.bn_stats(out, in_)
        self._commit("v", inst, [in_], [out])

    def bnaggr(self, out, in_):
        self._dep("v", [in_], [out])
        inst = self.nc.vector.bn_aggr(out, in_)
        self._commit("v", inst, [in_], [out])

    def recip(self, out, in_):
        self._dep("v", [in_], [out])
        inst = self.nc.vector.reciprocal(out, in_)
        self._commit("v", inst, [in_], [out])

    def memset(self, ap, c, eng="g"):
        self._dep(eng, [], [ap])
        inst = self._engine_obj(eng).memset(ap, c)
        self._commit(eng, inst, [], [ap])

    def dma(self, out, in_, q="qs"):
        eng = {"qs": "sy", "qg": "g"}[q]
        sub = f"{q}{self.qrr[q]}"
        self.qrr[q] = (self.qrr[q] + 1) % self.NSUB
        self._wait(eng, sub, self.cnt[sub])
        self._dep(eng, [in_], [out])
        inst = self._engine_obj(eng).dma_start(out, in_)
        self.cnt[sub] += 16
        inst.then_inc(self.sem[sub], 16)
        val = self.cnt[sub]
        nm, lo, hi = fp_of(in_)
        self._rec_access("r", nm, lo, hi, sub, val)
        nm, lo, hi = fp_of(out)
        self._rec_access("w", nm, lo, hi, sub, val)
        self.n_inst += 1

    def drain(self):
        for e in self.CENG:
            if self.cnt[e]:
                self._wait("g", e, self.cnt[e])
        for q in self.QENG:
            if self.cnt[q]:
                self._wait("g", q, self.cnt[q])



def host_consts():
    cc = np.zeros((128, 8), np.float32)
    cc[:, 0] = 1e-5
    cc[:, 1] = 0.125
    cc[:, 2] = 1.0
    cc[:, 3] = 1.0 / 127.0
    idn = np.eye(128, dtype=ml_dtypes.bfloat16)
    ones = np.ones((128, 128), dtype=ml_dtypes.bfloat16)
    eye64 = np.tile(np.eye(64, dtype=ml_dtypes.bfloat16), (2, 1))
    onesf = np.ones((1, 128), np.float32)
    return {"cc": cc, "idn": idn, "onesb": ones, "eye64": eye64, "onesf": onesf}


def decl_consts(p):
    d = {}
    d["cc_d"] = p.dram("cc", [128, 8], F32, kind="ExternalInput")
    d["idn_d"] = p.dram("idn", [128, 128], BF16, kind="ExternalInput")
    d["ones_d"] = p.dram("onesb", [128, 128], BF16, kind="ExternalInput")
    d["eye64_d"] = p.dram("eye64", [128, 64], BF16, kind="ExternalInput")
    d["onesf_d"] = p.dram("onesf", [1, 128], F32, kind="ExternalInput")
    d["cc"] = p.sb("cc_sb", (128, 8), F32)
    d["idn"] = p.sb("idn_sb", (128, 128), BF16)
    d["ones"] = p.sb("ones_sb", (128, 128), BF16)
    d["eye64"] = p.sb("eye64_sb", (128, 64), BF16)
    d["onesf"] = p.sb("onesf_sb", (1, 128), F32)
    p.dma(d["cc"][:], d["cc_d"], q="qs")
    p.dma(d["idn"][:], d["idn_d"], q="qs")
    p.dma(d["ones"][:], d["ones_d"], q="qs")
    p.dma(d["eye64"][:], d["eye64_d"], q="qs")
    p.dma(d["onesf"][:], d["onesf_d"], q="qs")
    d["eps"] = d["cc"][:, 0:1]
    d["eighth"] = d["cc"][:, 1:2]
    d["dequant"] = d["cc"][:, 3:4]
    return d


def load_vec_cols(p, name, vec_dram, n, q="qs"):
    t = p.sb(name, (128, n), F32)
    for i in range(n):
        src = vec_dram[i * 128:(i + 1) * 128].rearrange("(p o) -> p o", o=1)
        p.dma(t[:, i:i + 1], src, q=q)
    return t


class LnScr:
    def __init__(self, p, tag, n=4):
        self.n = n
        self.bst = [p.sb(f"{tag}_bst{i}", (128, 12), F32) for i in range(n)]
        self.agg = [p.sb(f"{tag}_agg{i}", (128, 2), F32) for i in range(n)]
        self.rstd = [p.sb(f"{tag}_rstd{i}", (128, 1), F32) for i in range(n)]
        self.nmu = [p.sb(f"{tag}_nmu{i}", (128, 1), F32) for i in range(n)]
        self.i = 0

    def next(self):
        i = self.i % self.n
        self.i += 1
        return self.bst[i], self.agg[i], self.rstd[i], self.nmu[i]


def ln_stats(p, C, scr, src_tile, ncols=1024):
    bst, agg, rstd, nmu = scr.next()
    nch = (ncols + 511) // 512
    for c in range(nch):
        p.bnstats(bst[:, 6 * c:6 * c + 6], src_tile[:, 512 * c:min(512 * (c + 1), ncols)])
    p.bnaggr(agg[:], bst[:, 0:6 * nch])
    p.act(nmu[:], agg[:, 1:2], AF.Ln, bias=C["eps"])
    p.act(rstd[:], nmu[:], AF.Exp, scale=-0.5)
    p.stt(nmu[:], agg[:, 0:1], -1.0, rstd[:], MULT, MULT)
    return rstd, nmu


def _ap(x):
    return x if isinstance(x, bass.AP) else x[:]


def epilogue_tile(p, C, scr, blk_tile, b, ti, x_dram, xo_dram, xbuf, lnbuf):
    xbuf = _ap(xbuf)
    lnbuf = _ap(lnbuf)
    rstd, nmu = ln_stats(p, C, scr, blk_tile)
    p.act(lnbuf, blk_tile, AF.Identity, bias=nmu[:, 0:1], scale=rstd[:, 0:1])
    p.dma(xbuf, x_dram[b, ti * 128:(ti + 1) * 128, :], q="qs")
    p.tt(xbuf, xbuf, lnbuf, ADD)
    p.dma(xo_dram[b, ti * 128:(ti + 1) * 128, :], xbuf, q="qg")



def declare_block0(p):
    return {
        "w_qkv": p.dram("w_qkv", [1024, 3072], F32, kind="ExternalInput"),
        "b_qkv": p.dram("b_qkv", [3072], F32, kind="ExternalInput"),
        "w_proj": p.dram("w_proj", [1024, 1024], F32, kind="ExternalInput"),
        "b_proj": p.dram("b_proj", [1024], F32, kind="ExternalInput"),
    }


def block0_body(p, C, scr, pf, pb, ins, x_d, xo_d, nbatch=2):
    wqkv_d = ins["w_qkv"]
    bqkv_d = ins["b_qkv"]
    wproj_d = ins["w_proj"]
    bproj_d = ins["b_proj"]

    bq = load_vec_cols(p, "bq", bqkv_d, 24)
    bp = load_vec_cols(p, "bp", bproj_d, 8)

    xnT = p.sb("xnT", (128, 16384), BF16)
    qT = [p.sb(f"qT{j}", (128, 2048), BF16) for j in range(8)]
    kT = [p.sb(f"kT{j}", (128, 2048), BF16) for j in range(8)]
    vtok = p.sb("vtok", (128, 16 * 1040), BF16)
    attnT = xnT
    eS = [p.sb(f"eS{h}", (128, 8 * 512), BF16) for h in range(4)]

    xt = [p.sb(f"xt{i}", (128, 1024), F32) for i in range(2)]
    xnt = [p.sb(f"xnt{i}", (128, 1024), BF16) for i in range(2)]
    xtA = xt + [eS[3][:].bitcast(F32)[:, 0:1024]]
    xntA = xnt + [eS[3][:, 2048:3072]]
    wst = [p.sb(f"wst{i}", (128, 128), F32) for i in range(4)]
    wqb = [p.sb(f"wqb{i}", (128, 128), BF16) for i in range(16)]
    wpb = [p.sb(f"wpb{i}", (128, 1024), BF16) for i in range(8)]

    pjt = [p.sb(f"pjt{i}", (128, 512), BF16) for i in range(2)]
    otokd = [eS[2], eS[3]]
    lnb = [eS[0][:].bitcast(F32)[:, 0:1024], eS[1][:].bitcast(F32)[:, 0:1024]]
    xrb = xt
    avs = [p.sb(f"avs{i}", (64, 512), BF16) for i in range(2)]
    wpf = xt[0][:]
    rden = lnb[0][0:1, 0:512]
    rrow = lnb[1][0:1, 0:512]

    for b in range(nbatch):
        prologue_to_xnT(p, C, scr, x_d, b, xnT, xtA, xntA, pb)

        for ft in range(24):
            wo = 8 * (ft % 2)
            for dt in range(8):
                ws = wst[dt % 4]
                p.dma(ws[:], wqkv_d[dt * 128:(dt + 1) * 128, ft * 128:(ft + 1) * 128], q="qs")
                p.copy(wqb[wo + dt][:], ws[:], eng="v")
            for tc in range(4):
                psq = pf[tc]
                for dt in range(8):
                    p.mm(psq, wqb[wo + dt][:], xnT[:, dt * 2048 + tc * 512: dt * 2048 + (tc + 1) * 512],
                         start=(dt == 0), stop=(dt == 7), commit=(dt == 7))
                if ft < 16:
                    if tc % 2 == 1:
                        dstT = qT[ft] if ft < 8 else kT[ft - 8]
                        p.evac(dstT[:, (tc - 1) * 512:(tc + 1) * 512],
                               p.pw[tc // 2][:], bias=bq[:, ft:ft + 1], eng="v")
                else:
                    vs = pjt[tc % 2]
                    p.evac(vs[:], psq, bias=bq[:, ft:ft + 1], eng="v")
                    hj = ft - 16
                    pbt = pb[tc % 2]
                    for q4 in range(4):
                        p.trans(pbt[:, q4 * 128:(q4 + 1) * 128],
                                vs[:, q4 * 128:(q4 + 1) * 128], C["idn"][:],
                                commit=(q4 == 3))
                    base = tc * 4 * 1040 + (2 * hj) * 65
                    dst = bass.AP(vtok, base, [[16640, 128], [1040, 4], [65, 2], [1, 64]])
                    srcv = pbt[:, 0:512].rearrange("p (q h f) -> p q h f", q=4, h=2)
                    p.copy(dst, srcv, eng="v")
        if b == 0:
            for kt in range(16):
                for h in range(16):
                    p.memset(vtok[:, kt * 1040 + h * 65 + 64: kt * 1040 + h * 65 + 65], 1.0, eng="g")

        for hp in range(8):
            for qc in range(4):
                for kh in range(2):
                    for k8 in range(8):
                        kt = kh * 8 + k8
                        ps1 = pf[k8 % 2]
                        ps2 = pf[2 + k8 % 2]
                        p.mm(ps1, kT[hp][0:64, kt * 128:(kt + 1) * 128],
                             qT[hp][0:64, qc * 512:(qc + 1) * 512], tp=(0, 0))
                        p.mm(ps2, kT[hp][64:128, kt * 128:(kt + 1) * 128],
                             qT[hp][64:128, qc * 512:(qc + 1) * 512], tp=(64, 0))
                        if k8 % 2 == 1:
                            p.act(eS[0 + 2 * kh][:, (k8 - 1) * 512:(k8 + 1) * 512],
                                  p.pw[0][:], AF.Exp, scale=C["eighth"])
                            p.act(eS[1 + 2 * kh][:, (k8 - 1) * 512:(k8 + 1) * 512],
                                  p.pw[1][:], AF.Exp, scale=C["eighth"])
                for kh in range(2):
                    for h01 in range(2):
                        h = hp * 2 + h01
                        psav = pf[4 + h01]
                        for k8 in range(8):
                            kt = kh * 8 + k8
                            p.mm(psav[0:65, :], vtok[:, kt * 1040 + h * 65: kt * 1040 + h * 65 + 65],
                                 eS[h01 + 2 * kh][:, k8 * 512:(k8 + 1) * 512],
                                 start=(kt == 0), stop=(kt == 15), commit=(k8 == 7))
                for h01 in range(2):
                    psav = pf[4 + h01]
                    p.act(rden, psav[64:65, :], AF.Ln)
                    p.act(rrow, rden, AF.Exp, scale=-1.0)
                    psden = pb[h01][:].bitcast(F32)
                    p.mm(psden[0:64, :], C["onesf"][0:1, 0:64], rrow)
                    av = avs[h01]
                    p.copy(av[:], psav[0:64, :], eng="v")
                    p.tt(attnT[64 * h01: 64 * h01 + 64, hp * 2048 + qc * 512: hp * 2048 + (qc + 1) * 512],
                         av[:], psden[0:64, :], MULT)

        for dt in range(8):
            p.dma(wpf, wproj_d[dt * 128:(dt + 1) * 128, :], q="qs")
            p.copy(wpb[dt][:], wpf, eng="v")
        for tc in range(4):
            for ot in range(8):
                psp = pf[ot % 4]
                for dt in range(8):
                    p.mm(psp, wpb[dt][:, ot * 128:(ot + 1) * 128],
                         attnT[:, dt * 2048 + tc * 512: dt * 2048 + (tc + 1) * 512],
                         start=(dt == 0), stop=(dt == 7), commit=(dt == 7))
                pj = pjt[ot % 2]
                p.evac(pj[:], psp, bias=bp[:, ot:ot + 1], eng="v")
                batched_out_tile(p, C, pb, ot, pj[:], otokd[0], ot, eng="v")
            for q4 in range(4):
                ti = tc * 4 + q4
                epilogue_tile(p, C, scr, otokd[0][:, q4 * 1024:(q4 + 1) * 1024], b, ti,
                              x_d, xo_d, xrb[q4 % 2], lnb[q4 % 2])


def common_pools(p):
    C = decl_consts(p)
    scr = LnScr(p, "ln")
    pw = [p.ps(f"pw{i}", (128, 1024), F32) for i in range(2)]
    pnx = p.ps("pnx", (128, 1024), F32)
    pb = [p.ps(f"pb{i}", (128, 1024), BF16) for i in range(2)]
    pf = [pw[0][:, 0:512], pw[0][:, 512:1024], pw[1][:, 0:512], pw[1][:, 512:1024],
          pnx[:, 0:512], pnx[:, 512:1024]]
    p.pw = pw
    p.pnx = pnx
    return C, scr, pf, pb


def build_block0(p, nbatch=2):
    x_d = p.dram("x", [nbatch, 2048, 1024], F32, kind="ExternalInput")
    xo_d = p.dram("xo", [nbatch, 2048, 1024], F32, kind="ExternalOutput")
    ins = declare_block0(p)
    C, scr, pf, pb = common_pools(p)
    p.push()
    block0_body(p, C, scr, pf, pb, ins, x_d, xo_d, nbatch)
    p.pop()
    p.drain()
    return p


def prologue_ti(p, C, scr, x_d, b, xnT, xt, xnt, pb, ti):
    xa = xt[ti % len(xt)]
    xna = xnt[ti % len(xnt)]
    xa_, xna_ = _ap(xa), _ap(xna)
    p.dma(xa_, x_d[b, ti * 128:(ti + 1) * 128, :], q="qs")
    rstd, nmu = ln_stats(p, C, scr, xa_)
    p.act(xna_, xa_, AF.Identity, bias=nmu[:, 0:1], scale=rstd[:, 0:1])
    pbt = pb[ti % 2]
    for j in range(8):
        p.trans(pbt[:, j * 128:(j + 1) * 128], xna[:, j * 128:(j + 1) * 128],
                C["idn"][:], commit=(j == 7))
    dst = bass.AP(xnT.tensor if isinstance(xnT, bass.AP) else xnT,
                  ti * 128, [[16384, 128], [2048, 8], [1, 128]])
    p.copy(dst, pbt[:].rearrange("p (j f) -> p j f", j=8), eng="v")


def prologue_to_xnT(p, C, scr, x_d, b, xnT, xt, xnt, pb):
    for ti in range(16):
        prologue_ti(p, C, scr, x_d, b, xnT, xt, xnt, pb, ti)


def batched_out_tile(p, C, pb, pbi, src_bf16_512, otok, ot, eng="s"):
    pbt = pb[pbi % 2]
    for q4 in range(4):
        p.trans(pbt[:, q4 * 128:(q4 + 1) * 128],
                src_bf16_512[:, q4 * 128:(q4 + 1) * 128], C["idn"][:],
                commit=(q4 == 3))
    dst = bass.AP(otok, ot * 128, [[4096, 128], [1024, 4], [1, 128]])
    p.copy(dst, pbt[:, 0:512].rearrange("p (q f) -> p q f", q=4), eng=eng)


def out_transpose_epilogue(p, C, scr, b, tc, evac_src_psum_fn, nout, x_d, xo_d,
                           pb, pjt, otok, lnb, xrb, pre_add_fn=None):
    for ot in range(nout):
        ps, bias, scale = evac_src_psum_fn(ot)
        pj = pjt[ot % 2]
        kw = {}
        if bias is not None:
            kw["bias"] = bias
        if scale is not None:
            kw["scale"] = scale
        p.act(pj[:], ps, AF.Identity, **kw)
        for q4 in range(4):
            pbt = pb[q4 % 2]
            p.trans(pbt[:, 0:128], pj[:, q4 * 128:(q4 + 1) * 128], C["idn"][:])
            p.copy(otok[q4][:, ot * 128:(ot + 1) * 128], pbt[:, 0:128], eng="s")
    for q4 in range(4):
        ti = tc * 4 + q4
        src = otok[q4][:]
        if pre_add_fn is not None:
            src = pre_add_fn(q4, ti)
        epilogue_tile(p, C, scr, src, b, ti, x_d, xo_d, xrb[q4 % 2], lnb[q4 % 2])



def declare_block1(p):
    return {
        "w_gate": p.dram("w_gate", [3072, 1024], I32, kind="ExternalInput"),
        "w_up": p.dram("w_up", [3072, 1024], I32, kind="ExternalInput"),
        "w_down": p.dram("w_down", [1024, 3072], I32, kind="ExternalInput"),
    }


def block1_body(p, C, scr, pf, pb, ins, x_d, xo_d, nbatch=2):
    wg_d = ins["w_gate"]
    wu_d = ins["w_up"]
    wd_d = ins["w_down"]

    xnT = p.sb("b1xnT", (128, 16384), BF16)
    hb = [p.sb(f"hb{j}", (128, 2048), BF16) for j in range(24)]
    xnt = [hb[23][:, 0:1024], hb[23][:, 1024:2048]]

    xt = [p.sb(f"xt{i}", (128, 1024), F32) for i in range(2)]
    wif = [p.sb(f"wif{i}", (128, 1024), I32) for i in range(2)]
    wib = [p.sb(f"wib{i}", (128, 1024), BF16) for i in range(2)]
    wgT = [p.sb(f"wgT{i}", (128, 128), BF16) for i in range(16)]
    wuT = [p.sb(f"wuT{i}", (128, 128), BF16) for i in range(16)]
    wdT = [p.sb(f"wdT{i}", (128, 128), BF16) for i in range(24)]
    sg1 = p.sb("sg1", (128, 1024), BF16)
    sdT = [p.sb(f"sdT{i}", (128, 2048), BF16) for i in range(8)]
    otok = p.sb("b1otok", (128, 4096), BF16)
    lnb = [xnT[:].bitcast(F32)[:, 0:1024], xnT[:].bitcast(F32)[:, 1024:2048]]
    xrb = xt

    s2col = p.sb("s2col", (128, 1), F32)
    p.act(s2col[:], C["dequant"], AF.Square)

    for b in range(nbatch):
        prologue_to_xnT(p, C, scr, x_d, b, xnT, xt, xnt, pb)

        for ht in range(24):
            wo = 8 * (ht % 2)
            for wk, (w_d, wT) in enumerate(((wg_d, wgT), (wu_d, wuT))):
                wi = wif[wk]
                wb = wib[wk]
                p.dma(wi[:], w_d[ht * 128:(ht + 1) * 128, :], q="qs")
                p.copy(wb[:], wi[:], eng="v")
                for j in range(8):
                    pbt = pb[j % 2]
                    p.trans(pbt[:, 0:128], wb[:, j * 128:(j + 1) * 128], C["idn"][:])
                    p.copy(wT[wo + j][:], pbt[:, 0:128], eng="s")
            for tc in range(4):
                psg = pf[0 + tc % 2]
                psu = pf[2 + tc % 2]
                for dtt in range(8):
                    p.mm(psg, wgT[wo + dtt][:], xnT[:, dtt * 2048 + tc * 512: dtt * 2048 + (tc + 1) * 512],
                         start=(dtt == 0), stop=(dtt == 7), commit=(dtt == 7))
                for dtt in range(8):
                    p.mm(psu, wuT[wo + dtt][:], xnT[:, dtt * 2048 + tc * 512: dtt * 2048 + (tc + 1) * 512],
                         start=(dtt == 0), stop=(dtt == 7), commit=(dtt == 7))
                if tc % 2 == 1:
                    p.act(sg1[:], p.pw[0][:], AF.Silu, scale=C["dequant"])
                    p.tt(hb[ht][:, (tc - 1) * 512:(tc + 1) * 512], sg1[:], p.pw[1][:], MULT)

        for dot in range(8):
            for c3 in range(3):
                wi = wif[c3 % 2]
                wb = wib[c3 % 2]
                p.dma(wi[:], wd_d[dot * 128:(dot + 1) * 128, c3 * 1024:(c3 + 1) * 1024], q="qs")
                p.copy(wb[:], wi[:], eng="v")
                for h8 in range(8):
                    ht = c3 * 8 + h8
                    pbt = pb[ht % 2]
                    p.trans(pbt[:, 0:128], wb[:, h8 * 128:(h8 + 1) * 128], C["idn"][:])
                    p.copy(wdT[ht][:], pbt[:, 0:128], eng="s")
            for tc in range(4):
                psd = pf[4 + tc % 2]
                for ht in range(24):
                    p.mm(psd, wdT[ht][:], hb[ht][:, tc * 512:(tc + 1) * 512],
                         start=(ht == 0), stop=(ht == 23), commit=(ht == 23))
                p.evac(sdT[dot][:, tc * 512:(tc + 1) * 512], psd,
                       scale=s2col[:, 0:1], eng="v")
        for tc in range(4):
            for ot in range(8):
                batched_out_tile(p, C, pb, ot, sdT[ot][:, tc * 512:(tc + 1) * 512],
                                 otok, ot, eng="v")
            for q4 in range(4):
                ti = tc * 4 + q4
                epilogue_tile(p, C, scr, otok[:, q4 * 1024:(q4 + 1) * 1024], b, ti,
                              x_d, xo_d, xrb[q4 % 2], lnb[q4 % 2])


def build_block1(p, nbatch=2):
    x_d = p.dram("x", [nbatch, 2048, 1024], F32, kind="ExternalInput")
    xo_d = p.dram("xo", [nbatch, 2048, 1024], F32, kind="ExternalOutput")
    ins = declare_block1(p)
    C, scr, pf, pb = common_pools(p)
    p.push()
    block1_body(p, C, scr, pf, pb, ins, x_d, xo_d, nbatch)
    p.pop()
    p.drain()
    return p



def declare_block3(p):
    return {
        "weight": p.dram("weight", [1024, 1024], F32, kind="ExternalInput"),
        "disha": p.dram("disha", [64, 1024], F32, kind="ExternalInput"),
    }


def block3_body(p, C, scr, pf, pb, ins, x_d, xo_d, nbatch=2):
    w_d = ins["weight"]
    dsh_d = ins["disha"]

    xnT = p.sb("b3xnT", (128, 16384), BF16)
    xnT2 = p.sb("b3xnT2", (128, 16384), BF16)
    xt = [p.sb(f"xt{i}", (128, 1024), F32) for i in range(2)]
    xnt = [p.sb(f"xnt{i}", (128, 1024), BF16) for i in range(2)]
    xt2 = [p.sb(f"x2t{i}", (128, 1024), F32) for i in range(2)]
    xnt2 = [p.sb(f"x2nt{i}", (128, 1024), BF16) for i in range(2)]
    scr2 = LnScr(p, "ln3b")
    wf = p.sb("wf", (128, 1024), F32)
    wb = p.sb("wb", (128, 1024), BF16)
    wT = [p.sb(f"wT{i}", (128, 1024), BF16) for i in range(8)]
    dsf = p.sb("dsf", (64, 1024), F32)
    dsb = p.sb("dsb", (64, 1024), BF16)
    poolb = p.sb("poolb", (64, 512), BF16)
    sdT = [p.sb(f"sdT{i}", (128, 2048), BF16) for i in range(8)]
    otok = p.sb("b3otok", (128, 4096), BF16)
    lnb = [p.sb(f"lnb{i}", (128, 1024), F32) for i in range(1)]
    xrb = xt

    for ot in range(8):
        p.dma(wf[:], w_d[ot * 128:(ot + 1) * 128, :], q="qs")
        p.copy(wb[:], wf[:], eng="v")
        for dt in range(8):
            pbt = pb[dt % 2]
            p.trans(pbt[:, 0:128], wb[:, dt * 128:(dt + 1) * 128], C["idn"][:])
            p.copy(wT[dt][:, ot * 128:(ot + 1) * 128], pbt[:, 0:128], eng="s")
    p.dma(dsf[:], dsh_d, q="qs")
    p.copy(dsb[:], dsf[:], eng="v")

    xnTs = [xnT, xnT2]
    prologue_to_xnT(p, C, scr, x_d, 0, xnT, xt, xnt, pb)
    for b in range(nbatch):
        xnTb = xnTs[b % 2]
        for tc in range(4):
            psp = pf[4]
            for dt in range(8):
                p.mm(psp[0:64, :], C["eye64"][:],
                     xnTb[:, dt * 2048 + tc * 512: dt * 2048 + (tc + 1) * 512],
                     start=(dt == 0), stop=(dt == 7), commit=(dt == 7))
            p.act(poolb[:], psp[0:64, :], AF.Identity)
            for ot in range(8):
                psr = pf[ot % 4]
                for dt in range(8):
                    p.mm(psr, wT[dt][:, ot * 128:(ot + 1) * 128],
                         xnTb[:, dt * 2048 + tc * 512: dt * 2048 + (tc + 1) * 512],
                         start=(dt == 0), stop=False, commit=False)
                p.mm(psr, dsb[0:64, ot * 128:(ot + 1) * 128], poolb[:],
                     start=False, stop=True, commit=True)
                p.act(sdT[ot][:, tc * 512:(tc + 1) * 512], psr, AF.Identity)
            for ot in range(8):
                batched_out_tile(p, C, pb, ot, sdT[ot][:, tc * 512:(tc + 1) * 512],
                                 otok, ot)
            for q4 in range(4):
                ti = tc * 4 + q4
                epilogue_tile(p, C, scr, otok[:, q4 * 1024:(q4 + 1) * 1024], b, ti,
                              x_d, xo_d, xrb[q4 % 2], lnb[0])
                if b + 1 < nbatch:
                    prologue_ti(p, C, scr2, x_d, b + 1, xnTs[(b + 1) % 2],
                                xt2, xnt2, pb, ti)


def build_block3(p, nbatch=2):
    x_d = p.dram("x", [nbatch, 2048, 1024], F32, kind="ExternalInput")
    xo_d = p.dram("xo", [nbatch, 2048, 1024], F32, kind="ExternalOutput")
    ins = declare_block3(p)
    C, scr, pf, pb = common_pools(p)
    p.push()
    block3_body(p, C, scr, pf, pb, ins, x_d, xo_d, nbatch)
    p.pop()
    p.drain()
    return p



def declare_block2(p, nbatch=2):
    return {
        "ln_g": p.dram("ln_g", [1024], F32, kind="ExternalInput"),
        "ln_b": p.dram("ln_b", [1024], F32, kind="ExternalInput"),
        "W_in": p.dram("W_in", [1024, 4096], F32, kind="ExternalInput"),
        "conv_w": p.dram("conv_w", [2048, 4], F32, kind="ExternalInput"),
        "conv_b": p.dram("conv_b", [2048], F32, kind="ExternalInput"),
        "W_x": p.dram("W_x", [2048, 96], F32, kind="ExternalInput"),
        "W_dt": p.dram("W_dt", [64, 2048], F32, kind="ExternalInput"),
        "b_dt": p.dram("b_dt", [2048], F32, kind="ExternalInput"),
        "A_log": p.dram("A_log", [2048, 16], F32, kind="ExternalInput"),
        "D": p.dram("D", [2048], F32, kind="ExternalInput"),
        "W_out": p.dram("W_out", [2048, 1024], F32, kind="ExternalInput"),
        "xin_s": p.dram("xin_s", [nbatch, 2048, 1024], BF16),
        "dt_sc": p.dram("dt_sc", [nbatch, 16, 128, 2048], BF16),
        "z_s": p.dram("z_s", [nbatch, 16, 128, 2048], BF16),
        "y_s": p.dram("y_s", [nbatch, 16, 128, 4096], BF16),
        "xh_s": p.dram("xh_s", [nbatch, 16, 128, 2048], BF16),
        "xc_s": p.dram("xc_s", [nbatch, 16, 128, 2048], BF16),
        "bc_s": p.dram("bc_s", [32, 2048], BF16),
    }


def block2_params(p, ins):
    P2 = {}
    P2["lg"] = load_vec_cols(p, "lg", ins["ln_g"], 8, q="qg")
    P2["lb"] = load_vec_cols(p, "lb", ins["ln_b"], 8, q="qg")
    P2["cb"] = load_vec_cols(p, "cbv", ins["conv_b"], 16, q="qg")
    P2["bdt"] = load_vec_cols(p, "bdtv", ins["b_dt"], 16, q="qg")
    P2["dcol"] = load_vec_cols(p, "dcol", ins["D"], 16, q="qg")

    cw = p.sb("cw", (128, 64), F32)
    A_sb = p.sb("A_sb", (128, 256), F32)
    astg = p.sb("astg", (128, 16), F32)
    for dt in range(16):
        p.dma(cw[:, dt * 4:(dt + 1) * 4], ins["conv_w"][dt * 128:(dt + 1) * 128, :], q="qg")
        p.dma(astg[:], ins["A_log"][dt * 128:(dt + 1) * 128, :], q="qg")
        p.act(A_sb[:, dt * 16:(dt + 1) * 16], astg[:], AF.Exp)
        p.ts(A_sb[:, dt * 16:(dt + 1) * 16], A_sb[:, dt * 16:(dt + 1) * 16], -1.0, MULT)
    P2["cw"] = cw
    P2["A_sb"] = A_sb
    return P2


def block2_body(p, C, scr, pf, pb, ins, x_d, xo_d, nbatch=2, P2=None):
    win_d = ins["W_in"]
    wout_d = ins["W_out"]
    xin_s = ins["xin_s"]; z_s = ins["z_s"]; y_s = ins["y_s"]; dt_sc = ins["dt_sc"]
    xh_s = ins["xh_s"]; xc_s = ins["xc_s"]

    if P2 is None:
        P2 = block2_params(p, ins)
    lg = P2["lg"]; lb = P2["lb"]; cb = P2["cb"]; bdt = P2["bdt"]
    dcol = P2["dcol"]; cw = P2["cw"]; A_sb = P2["A_sb"]

    Wx_sb = p.sb("Wx_sb", (128, 16 * 96), BF16)
    wxf = p.sb("wxf", (128, 96), F32)
    for dt in range(16):
        p.dma(wxf[:], ins["W_x"][dt * 128:(dt + 1) * 128, :], q="qs")
        p.copy(Wx_sb[:, dt * 96:(dt + 1) * 96], wxf[:], eng="v")

    wdtstg = p.sb("wdtstg", (64, 1024), F32)
    Wdt_sb = p.sb("Wdt_sb", (64, 2048), BF16)
    for h in range(2):
        p.dma(wdtstg[:], ins["W_dt"][:, h * 1024:(h + 1) * 1024], q="qs")
        p.copy(Wdt_sb[:, h * 1024:(h + 1) * 1024], wdtstg[:], eng="v")

    bfB = [p.sb(f"bfB{i}", (128, 2048), BF16) for i in range(8)]
    bfC = [p.sb(f"bfC{i}", (128, 2048), BF16) for i in range(8)]
    w8 = [p.sb(f"w8_{i}", (128, 2048), BF16) for i in range(8)]
    xcs = [p.sb(f"xcs{i}", (128, 2052), BF16) for i in range(2)]
    fA = [p.sb(f"fA{i}", (128, 2048), F32) for i in range(3)]
    yq = [p.sb(f"yq{i}", (128, 2048), BF16) for i in range(2)]
    fB = [p.sb(f"fB{i}", (128, 2048), BF16) for i in range(6)]
    at3 = p.sb("at3", (128, 2048), BF16)
    hcb = [fB[5], fB[5]]
    du2 = [fB[0], p.sb("du1", (128, 2048), BF16)]
    dbc = p.sb("dbc", (96, 2048), BF16)
    brow = [wdtstg[0:1, :].bitcast(BF16)]
    wpl = bfB + w8

    xt = [fA[0][:, 0:1024], fA[1][:, 0:1024], fA[2][:, 0:1024], fA[2][:, 1024:2048]]
    xnt = [fB[1][:, 0:1024], fB[1][:, 1024:2048], fB[2][:, 0:1024], fB[2][:, 1024:2048]]
    xint = [p.sb(f"xint{i}", (128, 1024), BF16) for i in range(2)]
    xint3 = xint + [at3[:, 0:1024], at3[:, 1024:2048]]
    wst = [fB[5][:].bitcast(F32)[:, 128 * i:128 * (i + 1)] for i in range(4)]
    wsb = [fB[2][:, 128 * i:128 * (i + 1)] for i in range(8)] + [fB[3][:, 128 * i:128 * (i + 1)] for i in range(8)]
    zst = [fB[0][:, 0:512], fB[0][:, 512:1024]]
    pjt = [fB[0][:, 1024:1536], fB[0][:, 1536:2048]]

    for b in range(nbatch):
        for ti in range(16):
            xa = xt[ti % 4]
            xna = xnt[ti % 4]
            xia = xint3[ti % 4]
            p.dma(xa, x_d[b, ti * 128:(ti + 1) * 128, :], q="qs")
            rstd, nmu = ln_stats(p, C, scr, xa)
            p.act(xia[:], xa, AF.Identity, bias=nmu[:, 0:1], scale=rstd[:, 0:1])
            p.dma(xin_s[b, ti * 128:(ti + 1) * 128, :], xia[:], q="qg")
            rstd, nmu = ln_stats(p, C, scr, xia[:])
            p.act(xna, xia[:], AF.Identity, bias=nmu[:, 0:1], scale=rstd[:, 0:1])
            for j in range(8):
                pbt = pb[j % 2]
                p.trans(pbt[:, 0:128], xna[:, j * 128:(j + 1) * 128], C["idn"][:])
                p.act(bfB[j][:, ti * 128:(ti + 1) * 128], pbt[:, 0:128], AF.Identity,
                      scale=lg[:, j:j + 1], bias=lb[:, j:j + 1])

        for ot in range(32):
            wo = 8 * (ot % 2)
            for dt8 in range(8):
                ws = wst[dt8 % 4]
                p.dma(ws, win_d[dt8 * 128:(dt8 + 1) * 128, ot * 128:(ot + 1) * 128], q="qs")
                p.copy(wsb[wo + dt8], ws, eng="v")
            for tc in range(4):
                ps = pf[tc]
                for dt8 in range(8):
                    p.mm(ps, wsb[wo + dt8], bfB[dt8][:, tc * 512:(tc + 1) * 512],
                         start=(dt8 == 0), stop=(dt8 == 7), commit=(dt8 == 7))
                if tc % 2 == 1:
                    if ot < 16:
                        xq = yq[ot % 2]
                        p.act(xq[:, (tc - 1) * 512:(tc + 1) * 512],
                              p.pw[tc // 2][:], AF.Identity)
                        if tc == 3:
                            p.dma(xh_s[b, ot], xq[:], q="qg")
                    else:
                        zt = fB[0][:, 0:1024] if tc == 1 else fB[0][:, 1024:2048]
                        p.act(zt, p.pw[tc // 2][:], AF.Identity)
                        p.dma(z_s[b, ot - 16, :, (tc - 1) * 512:(tc + 1) * 512], zt, q="qg")

        psdc = [pf[4], pf[5], pb[0][:].bitcast(F32), pb[1][:].bitcast(F32)]
        for dt in range(16):
            xc = xcs[dt % 2]
            p.dma(xc[:, 3:2051], xh_s[b, dt], q="qs")
            c0 = fA[0] if dt % 2 == 0 else fA[2]
            c1 = fA[1] if dt % 2 == 0 else fA[0]
            p.memset(xc[:, 0:3], 0, eng="g")
            p.ts(c0[:], xc[:, 3:2051], cw[:, dt * 4 + 3: dt * 4 + 4], MULT,
                 s2=cb[:, dt:dt + 1], op1=ADD)
            p.stt(c1[:], xc[:, 2:2050], cw[:, dt * 4 + 2: dt * 4 + 3], c0[:], MULT, ADD)
            p.stt(c0[:], xc[:, 1:2049], cw[:, dt * 4 + 1: dt * 4 + 2], c1[:], MULT, ADD)
            p.stt(c1[:], xc[:, 0:2048], cw[:, dt * 4 + 0: dt * 4 + 1], c0[:], MULT, ADD)
            p.act(xc[:, 3:2051], c1[:], AF.Silu)
            p.dma(xc_s[b, dt], xc[:, 3:2051], q="qg")
            for c in range(4):
                p.mm(psdc[c][0:96, :], Wx_sb[:, dt * 96:(dt + 1) * 96],
                     xc[:, 3 + c * 512: 3 + (c + 1) * 512],
                     start=(dt == 0), stop=(dt == 15), commit=(c == 3))
        for c in range(4):
            p.copy(dbc[64:96, c * 512:(c + 1) * 512], psdc[c][64:96, :], eng="s")
        p.dma(ins["bc_s"], dbc[64:96, :], q="qg")
        for c in range(4):
            p.copy(dbc[0:64, c * 512:(c + 1) * 512], psdc[c][0:64, :], eng="s")

        yacc = [pf[4], pf[5], pb[0][:].bitcast(F32), pb[1][:].bitcast(F32)]
        dtb = [fA[0][:].bitcast(BF16)[:, 0:2048], fA[0][:].bitcast(BF16)[:, 2048:4096]]
        aTb = [fA[2][:].bitcast(BF16)[:, 0:2048], fA[2][:].bitcast(BF16)[:, 2048:4096],
               at3[:]]
        def bcast_bc(nh, n8, which):
            roff, dest, q = (0, bfB, "qs") if which == 0 else (16, bfC, "qg")
            src = bass.AP(ins["bc_s"].tensor, (roff + nh * 8 + n8) * 2048,
                          [[0, 128], [1, 2048]])
            p.dma(dest[n8][:], src, q=q)

        for nh in range(2):
            if nh == 0:
                for n8 in range(8):
                    bcast_bc(0, n8, 0)
                    bcast_bc(0, n8, 1)
            if nh == 0:
                p.dma(xcs[0][:, 3:2051], xc_s[b, 0], q="qs")

            def emit_head(dtn, stage):
                if dtn >= 16:
                    return
                dtcn = dtb[dtn % 2]
                if nh == 0:
                    if stage in (0, 1, 2, 3) and stage % 2 == 0:
                        for c in (stage, stage + 1):
                            psq = pf[2 + c % 2]
                            p.mm(psq[:], Wdt_sb[0:64, dtn * 128:(dtn + 1) * 128],
                                 dbc[0:64, c * 512:(c + 1) * 512])
                    elif stage in (1, 3):
                        for c in (stage - 1, stage):
                            psq = pf[2 + c % 2]
                            p.act(fA[1][:, c * 512:(c + 1) * 512], psq[:], AF.Exp,
                                  bias=bdt[:, dtn:dtn + 1])
                    elif stage == 4:
                        p.act(dtcn, fA[1][:], AF.Ln, bias=C["cc"][:, 2:3])
                        p.dma(dt_sc[b, dtn], dtcn, q="qg")
                    else:
                        p.tt(du2[dtn % 2][:], dtcn, xcs[dtn % 2][:, 3:2051],
                             MULT, eng="g")
                else:
                    if stage == 0:
                        p.dma(dtcn, dt_sc[b, dtn], q="qs")
                        p.dma(yq[dtn % 2][:], y_s[b, dtn, :, 0:2048], q="qs")
                    elif stage == 5:
                        p.tt(du2[dtn % 2][:], dtcn, xcs[dtn % 2][:, 3:2051],
                             MULT, eng="g")

            for st in range(6):
                if nh == 1 and st == 0:
                    continue
                emit_head(0, st)
            tail_prev = None
            for dt in range(16):
                dtc = dtb[dt % 2]
                xc = xcs[dt % 2]
                du = du2[dt % 2]

                def hc_emit(n8):
                    hT_ = fB[3 + n8 % 2]
                    hc = hcb[n8 % 2]
                    p.tt(hc[:], hT_[:], bfC[n8][:], MULT, eng="v")
                    for c in range(4):
                        p.mm(yacc[c], C["idn"][:], hc[:, c * 512:(c + 1) * 512],
                             start=(nh == 0 and n8 == 0), stop=(n8 == 7), commit=(c == 3))

                for n8 in range(8):
                    nn = nh * 8 + n8
                    aT = aTb[n8 % 3]
                    vT = fB[1 + n8 % 2]
                    hT = fB[3 + n8 % 2]
                    p.act(aT, dtc, AF.Exp, scale=A_sb[:, dt * 16 + nn: dt * 16 + nn + 1])
                    p.tt(vT[:], du[:], bfB[n8][:], MULT, eng="v")
                    p.scan(hT[:], aT, vT[:], 0.0)
                    if nh == 0 and dt == 15:
                        bcast_bc(1, n8, 0)
                        if n8 == 4:
                            p.dma(xcs[0][:, 3:2051], xc_s[b, 0], q="qs")
                            p.dma(dtb[0], dt_sc[b, 0], q="qs")
                            p.dma(yq[0][:], y_s[b, 0, :, 0:2048], q="qs")
                    if n8 == 0 and tail_prev is not None:
                        tail_prev()
                    if n8 == 0 and nh == 1:
                        y0s = yq[dt % 2]
                        for c in range(4):
                            p.mm(yacc[c], C["idn"][:], y0s[:, c * 512:(c + 1) * 512],
                                 start=True, stop=False, commit=(c == 3))
                    if n8 == 1 and dt + 1 < 16:
                        p.dma(xcs[(dt + 1) % 2][:, 3:2051], xc_s[b, dt + 1], q="qs")
                    if n8 >= 1:
                        hc_emit(n8 - 1)
                        if nh == 0 and dt == 15:
                            bcast_bc(1, n8 - 1, 1)
                    if 1 <= n8 <= 4:
                        emit_head(dt + 1, n8 - 1)
                    elif n8 == 5:
                        emit_head(dt + 1, 4)
                    elif n8 == 6:
                        emit_head(dt + 1, 5)
                hc_emit(7)
                if nh == 0 and dt == 15:
                    bcast_bc(1, 7, 1)

                def make_tail(dtp, xcp, nhp):
                    def tail():
                        if nhp == 0:
                            yo = yq[dtp % 2]
                            p.copy(yo[:, 0:1024], p.pnx[:, 0:1024], eng="v")
                            for c in (2, 3):
                                p.copy(yo[:, c * 512:(c + 1) * 512], yacc[c], eng="v")
                            p.dma(y_s[b, dtp, :, 0:2048], yo[:], q="qg")
                        else:
                            p.stt(at3[:, 0:1024], xcp[:, 3:1027],
                                  dcol[:, dtp:dtp + 1], p.pnx[:, 0:1024], MULT, ADD)
                            for c in (2, 3):
                                p.stt(at3[:, c * 512:(c + 1) * 512],
                                      xcp[:, 3 + c * 512: 3 + (c + 1) * 512],
                                      dcol[:, dtp:dtp + 1], yacc[c], MULT, ADD)
                            p.dma(y_s[b, dtp, :, 2048:4096], at3[:], q="qg")
                    return tail

                tail_prev = make_tail(dt, xc, nh)
            tail_prev()
            tail_prev = None

        for dt in range(16):
            yb = fB[1 + dt % 2]
            zb = fB[3 + dt % 2]
            p.dma(yb[:], y_s[b, dt, :, 2048:4096], q="qs")
            p.dma(zb[:], z_s[b, dt], q="qs")
            p.act(hcb[dt % 2][:], zb[:], AF.Silu)
            p.tt(wpl[dt][:], yb[:], hcb[dt % 2][:], MULT)
            wos = fA[0][:, (dt % 2) * 1024:(dt % 2) * 1024 + 1024]
            p.dma(wos, wout_d[dt * 128:(dt + 1) * 128, :], q="qg")
            p.copy(bfC[dt // 2][:, (dt % 2) * 1024:(dt % 2 + 1) * 1024],
                   wos, eng="v")
        for tc in range(4):
            for ot in range(8):
                ps = pf[ot % 2]
                for dt in range(16):
                    p.mm(ps[:], bfC[dt // 2][:, (dt % 2) * 1024 + ot * 128:
                                             (dt % 2) * 1024 + (ot + 1) * 128],
                         wpl[dt][:, tc * 512:(tc + 1) * 512],
                         start=(dt == 0), stop=(dt == 15), commit=(dt == 15))
                pj = pjt[ot % 2]
                p.act(pj, ps[:], AF.Identity)
                for q4 in range(4):
                    pbt = pb[q4 % 2]
                    p.trans(pbt[:, 0:128], pj[:, q4 * 128:(q4 + 1) * 128], C["idn"][:])
                    p.copy(fB[2 + q4 // 2][:, (q4 % 2) * 1024 + ot * 128:
                                           (q4 % 2) * 1024 + (ot + 1) * 128],
                           pbt[:, 0:128], eng="s")
            for q4 in range(4):
                ti = tc * 4 + q4
                mo = fB[2 + q4 // 2][:, (q4 % 2) * 1024:(q4 % 2) * 1024 + 1024]
                xia = xint[q4 % 2]
                p.dma(xia[:], xin_s[b, ti * 128:(ti + 1) * 128, :], q="qs")
                mtile = fA[2][:, 1024 * (q4 % 2):1024 * (q4 % 2) + 1024]
                p.tt(mtile, mo, xia[:], ADD)
                epilogue_tile(p, C, scr, mtile, b, ti, x_d, xo_d,
                              fA[0][:, 1024 * (q4 % 2):1024 * (q4 % 2) + 1024],
                              yq[q4 % 2][:].bitcast(F32)[:, 0:1024])


def build_block2(p, nbatch=2):
    x_d = p.dram("x", [nbatch, 2048, 1024], F32, kind="ExternalInput")
    xo_d = p.dram("xo", [nbatch, 2048, 1024], F32, kind="ExternalOutput")
    ins = declare_block2(p, nbatch)
    C, scr, pf, pb = common_pools(p)
    p.push()
    block2_body(p, C, scr, pf, pb, ins, x_d, xo_d, nbatch)
    p.pop()
    p.drain()
    return p


def build_fused(p, nbatch=2):
    x_d = p.dram("x", [nbatch, 2048, 1024], F32, kind="ExternalInput")
    xo_d = p.dram("xo", [nbatch, 2048, 1024], F32, kind="ExternalOutput")
    x1_d = p.dram("x1s", [nbatch, 2048, 1024], F32)
    x2_d = p.dram("x2s", [nbatch, 2048, 1024], F32)
    x3_d = p.dram("x3s", [nbatch, 2048, 1024], F32)
    ins0 = declare_block0(p)
    ins1 = declare_block1(p)
    ins2 = declare_block2(p, nbatch)
    ins3 = declare_block3(p)
    C, scr, pf, pb = common_pools(p)
    P2 = block2_params(p, ins2)
    p.push(); block0_body(p, C, scr, pf, pb, ins0, x_d, x1_d, nbatch); p.pop()
    p.barrier()
    p.push(); block1_body(p, C, scr, pf, pb, ins1, x1_d, x2_d, nbatch); p.pop()
    p.barrier()
    p.push(); block2_body(p, C, scr, pf, pb, ins2, x2_d, x3_d, nbatch, P2=P2); p.pop()
    p.barrier()
    p.push(); block3_body(p, C, scr, pf, pb, ins3, x3_d, xo_d, nbatch); p.pop()
    p.drain()
    return p



N_CORES = 8


def kernel(**inputs):
    consts = host_consts()
    xs = np.ascontiguousarray(inputs["L0_attn_x"])
    shards = [xs[2 * c:2 * c + 2] for c in range(N_CORES)]

    p = P()
    build_fused(p, nbatch=2)
    with p.es:
        im = []
        for c in range(N_CORES):
            d = dict(consts)
            d.update({
                "x": shards[c],
                "w_qkv": inputs["L0_attn_w_qkv"],
                "b_qkv": inputs["L0_attn_b_qkv"],
                "w_proj": inputs["L0_attn_w_proj"],
                "b_proj": inputs["L0_attn_b_proj"],
                "w_gate": inputs["L0_ffn_w_gate"],
                "w_up": inputs["L0_ffn_w_up"],
                "w_down": inputs["L0_ffn_w_down"],
                "ln_g": inputs["L1_attn_ln_g"], "ln_b": inputs["L1_attn_ln_b"],
                "W_in": inputs["L1_attn_W_in"], "conv_w": inputs["L1_attn_conv_w"],
                "conv_b": inputs["L1_attn_conv_b"], "W_x": inputs["L1_attn_W_x"],
                "W_dt": inputs["L1_attn_W_dt"], "b_dt": inputs["L1_attn_b_dt"],
                "A_log": inputs["L1_attn_A_log"], "D": inputs["L1_attn_D"],
                "W_out": inputs["L1_attn_W_out"],
                "weight": inputs["L1_ffn_weight"],
                "disha": inputs["L1_ffn_disha"],
            })
            im.append(d)
        res = run_bass_kernel_spmd(p.nc, im, core_ids=list(range(N_CORES)))
    return np.concatenate([r["xo"] for r in res.results], axis=0).astype(np.float32)


# version 42 (final)
# speedup vs baseline: 1.0128x; 1.0008x over previous
import contextlib
import numpy as np
import ml_dtypes
import concourse.bass as bass
import concourse.mybir as mybir
from concourse.bass_utils import run_bass_kernel_spmd

F32 = mybir.dt.float32
BF16 = mybir.dt.bfloat16
I32 = mybir.dt.int32
AF = mybir.ActivationFunctionType
ALU = mybir.AluOpType
AX = mybir.AxisListType

MULT = ALU.mult
ADD = ALU.add


def fp_of(ap):
    lo = ap.offset
    hi = ap.offset
    for st, ct in ap.ap:
        d = st * (ct - 1)
        if d < 0:
            lo += d
        else:
            hi += d
    return (ap.tensor.name, lo, hi + 1)


class P:

    CENG = ("t", "v", "s", "g")
    NSUB = 16
    QENG = tuple(f"qs{i}" for i in range(16)) + tuple(f"qg{i}" for i in range(16))

    def __init__(self):
        self.nc = bass.Bass("TRN2", target_bir_lowering=False)
        self.es = contextlib.ExitStack()
        self.stacks = [self.es]
        self.sem = {}
        self.cnt = {}
        for e in self.CENG + self.QENG:
            self.sem[e] = self.es.enter_context(self.nc.semaphore("sem_" + e))
            self.cnt[e] = 0
        self.qrr = {"qs": 0, "qg": 0}
        self.waited = {e: {} for e in self.CENG + ("sy",)}
        self.rec = {}
        self.pe_pending = []
        self.n_inst = 0
        self._names = set()

    def _uniq(self, name):
        if name in self._names:
            k = 2
            while f"{name}_{k}" in self._names:
                k += 1
            name = f"{name}_{k}"
        self._names.add(name)
        return name

    def sb(self, name, shape, dtype):
        name = self._uniq(name)
        return self.stacks[-1].enter_context(self.nc.sbuf_tensor(name, list(shape), dtype))

    def ps(self, name, shape, dtype):
        return self.stacks[-1].enter_context(self.nc.psum_tensor(name, list(shape), dtype))

    def push(self):
        self.stacks.append(contextlib.ExitStack())

    def pop(self):
        self.stacks.pop().close()

    def barrier(self):
        for e in ("t", "v", "s", "g", "sy"):
            for sn in self.CENG + self.QENG:
                if self.cnt[sn] and not (e == sn):
                    self._wait(e, sn, self.cnt[sn])

    def dram(self, name, shape, dtype, kind=None):
        if kind:
            return self.nc.dram_tensor(name, list(shape), dtype, kind=kind).ap()
        return self.nc.dram_tensor(name, list(shape), dtype).ap()

    def _engine_obj(self, e):
        return {"t": self.nc.tensor, "v": self.nc.vector, "s": self.nc.scalar,
                "g": self.nc.gpsimd, "sy": self.nc.sync}[e]

    def _wait(self, eng, sname, val):
        w = self.waited[eng]
        if w.get(sname, 0) >= val:
            return
        self._engine_obj(eng).wait_ge(self.sem[sname], val)
        w[sname] = val

    def _dep(self, eng, reads, writes):
        own = {"t": "t", "v": "v", "s": "s", "g": "g", "sy": None}[eng]
        for ap in reads:
            nm, lo, hi = fp_of(ap)
            r = self.rec.get(nm)
            if not r:
                continue
            for (l2, h2, sname, val) in r["w"]:
                if l2 < hi and lo < h2:
                    if eng == "t" and sname == "t":
                        continue
                    self._wait(eng, sname, val)
        for ap in writes:
            nm, lo, hi = fp_of(ap)
            r = self.rec.get(nm)
            if not r:
                continue
            for (l2, h2, sname, val) in r["w"]:
                if l2 < hi and lo < h2:
                    if eng == "t" and sname == "t":
                        continue
                    self._wait(eng, sname, val)
            for (l2, h2, sname, val) in r["r"]:
                if l2 < hi and lo < h2:
                    if eng == "t" and sname == "t":
                        continue
                    self._wait(eng, sname, val)

    def _rec_access(self, kind, nm, lo, hi, sname, val):
        r = self.rec.setdefault(nm, {"w": [], "r": []})
        lst = r[kind]
        if kind == "w":
            r["w"] = [x for x in r["w"] if not (lo <= x[0] and x[1] <= hi)]
            r["r"] = [x for x in r["r"] if not (lo <= x[0] and x[1] <= hi)]
            lst = r["w"]
        lst.append((lo, hi, sname, val))
        if len(lst) > 96:
            del lst[0:32]

    def _commit(self, eng, inst, reads, writes, inc=True):
        self.n_inst += 1
        if eng in ("t",):
            if inc:
                self.cnt["t"] += 1
                inst.then_inc(self.sem["t"], 1)
                val = self.cnt["t"]
                for (kind, nm, lo, hi) in self.pe_pending:
                    self._rec_access(kind, nm, lo, hi, "t", val)
                self.pe_pending = []
                for ap in reads:
                    nm, lo, hi = fp_of(ap)
                    self._rec_access("r", nm, lo, hi, "t", val)
                for ap in writes:
                    nm, lo, hi = fp_of(ap)
                    self._rec_access("w", nm, lo, hi, "t", val)
            else:
                for ap in reads:
                    nm, lo, hi = fp_of(ap)
                    self.pe_pending.append(("r", nm, lo, hi))
                for ap in writes:
                    nm, lo, hi = fp_of(ap)
                    self.pe_pending.append(("w", nm, lo, hi))
        else:
            self.cnt[eng] += 1
            inst.then_inc(self.sem[eng], 1)
            val = self.cnt[eng]
            for ap in reads:
                nm, lo, hi = fp_of(ap)
                self._rec_access("r", nm, lo, hi, eng, val)
            for ap in writes:
                nm, lo, hi = fp_of(ap)
                self._rec_access("w", nm, lo, hi, eng, val)

    def mm(self, out, lhsT, rhs, start=True, stop=True, tp=None, commit=True):
        self._dep("t", [lhsT, rhs], [out])
        kw = {}
        if tp is not None:
            kw["tile_position"] = tp
        inst = self.nc.tensor.matmul(out, lhsT, rhs, start=start, stop=stop, **kw)
        self._commit("t", inst, [lhsT, rhs], [out], inc=commit)

    def trans(self, out_ps, in_sb, ident, commit=True):
        self._dep("t", [in_sb, ident], [out_ps])
        inst = self.nc.tensor.transpose(out_ps, in_sb, ident)
        self._commit("t", inst, [in_sb, ident], [out_ps], inc=commit)

    def act(self, out, in_, func, bias=0.0, scale=1.0, accum_out=None):
        reads = [in_] + [x for x in (bias, scale) if isinstance(x, bass.AP)]
        writes = [out] + ([accum_out] if accum_out is not None else [])
        self._dep("s", reads, writes)
        inst = self.nc.scalar.activation(out, in_, func, bias=bias, scale=scale,
                                         accum_out=accum_out)
        self._commit("s", inst, reads, writes)

    def tt(self, out, a, b, op, eng="v"):
        self._dep(eng, [a, b], [out])
        inst = self._engine_obj(eng).tensor_tensor(out, a, b, op)
        self._commit(eng, inst, [a, b], [out])

    def ts(self, out, in0, s1, op0, s2=None, op1=ALU.bypass, eng="v", accum_out=None):
        reads = [in0] + [x for x in (s1, s2) if isinstance(x, bass.AP)]
        writes = [out] + ([accum_out] if accum_out is not None else [])
        self._dep(eng, reads, writes)
        inst = self._engine_obj(eng).tensor_scalar(out, in0, s1, s2, op0, op1,
                                                   accum_out=accum_out)
        self._commit(eng, inst, reads, writes)

    def stt(self, out, in0, scalar, in1, op0, op1, eng="v"):
        reads = [in0, in1] + ([scalar] if isinstance(scalar, bass.AP) else [])
        self._dep(eng, reads, [out])
        inst = self._engine_obj(eng).scalar_tensor_tensor(out, in0, scalar, in1, op0, op1)
        self._commit(eng, inst, reads, [out])

    def scan(self, out, d0, d1, initial=0.0):
        reads = [d0, d1] + ([initial] if isinstance(initial, bass.AP) else [])
        self._dep("v", reads, [out])
        inst = self.nc.vector.tensor_tensor_scan(out, d0, d1, initial, MULT, ADD)
        self._commit("v", inst, reads, [out])

    def evac(self, out, psum, bias=None, scale=None, eng="s"):
        if eng == "s":
            kw = {}
            if bias is not None:
                kw["bias"] = bias
            if scale is not None:
                kw["scale"] = scale
            return self.act(out, psum, AF.Identity, **kw)
        if scale is not None and bias is not None:
            return self.ts(out, psum, scale, MULT, s2=bias, op1=ADD, eng=eng)
        if scale is not None:
            return self.ts(out, psum, scale, MULT, eng=eng)
        if bias is not None:
            return self.ts(out, psum, bias, ADD, eng=eng)
        return self.copy(out, psum, eng=eng)

    def copy(self, out, in_, eng="v"):
        if eng == "s":
            return self.act(out, in_, AF.Identity)
        self._dep(eng, [in_], [out])
        inst = self._engine_obj(eng).tensor_copy(out, in_)
        self._commit(eng, inst, [in_], [out])

    def red(self, out, in_, op, eng="v"):
        self._dep(eng, [in_], [out])
        inst = self._engine_obj(eng).tensor_reduce(out, in_, AX.X, op)
        self._commit(eng, inst, [in_], [out])

    def bnstats(self, out, in_):
        self._dep("v", [in_], [out])
        inst = self.nc.vector.bn_stats(out, in_)
        self._commit("v", inst, [in_], [out])

    def bnaggr(self, out, in_):
        self._dep("v", [in_], [out])
        inst = self.nc.vector.bn_aggr(out, in_)
        self._commit("v", inst, [in_], [out])

    def recip(self, out, in_):
        self._dep("v", [in_], [out])
        inst = self.nc.vector.reciprocal(out, in_)
        self._commit("v", inst, [in_], [out])

    def memset(self, ap, c, eng="g"):
        self._dep(eng, [], [ap])
        inst = self._engine_obj(eng).memset(ap, c)
        self._commit(eng, inst, [], [ap])

    def dma(self, out, in_, q="qs"):
        eng = {"qs": "sy", "qg": "g"}[q]
        sub = f"{q}{self.qrr[q]}"
        self.qrr[q] = (self.qrr[q] + 1) % self.NSUB
        self._wait(eng, sub, self.cnt[sub])
        self._dep(eng, [in_], [out])
        inst = self._engine_obj(eng).dma_start(out, in_)
        self.cnt[sub] += 16
        inst.then_inc(self.sem[sub], 16)
        val = self.cnt[sub]
        nm, lo, hi = fp_of(in_)
        self._rec_access("r", nm, lo, hi, sub, val)
        nm, lo, hi = fp_of(out)
        self._rec_access("w", nm, lo, hi, sub, val)
        self.n_inst += 1

    def drain(self):
        for e in self.CENG:
            if self.cnt[e]:
                self._wait("g", e, self.cnt[e])
        for q in self.QENG:
            if self.cnt[q]:
                self._wait("g", q, self.cnt[q])



def host_consts():
    cc = np.zeros((128, 8), np.float32)
    cc[:, 0] = 1e-5
    cc[:, 1] = 0.125
    cc[:, 2] = 1.0
    cc[:, 3] = 1.0 / 127.0
    idn = np.eye(128, dtype=ml_dtypes.bfloat16)
    ones = np.ones((128, 128), dtype=ml_dtypes.bfloat16)
    eye64 = np.tile(np.eye(64, dtype=ml_dtypes.bfloat16), (2, 1))
    onesf = np.ones((1, 128), np.float32)
    return {"cc": cc, "idn": idn, "onesb": ones, "eye64": eye64, "onesf": onesf}


def decl_consts(p):
    d = {}
    d["cc_d"] = p.dram("cc", [128, 8], F32, kind="ExternalInput")
    d["idn_d"] = p.dram("idn", [128, 128], BF16, kind="ExternalInput")
    d["ones_d"] = p.dram("onesb", [128, 128], BF16, kind="ExternalInput")
    d["eye64_d"] = p.dram("eye64", [128, 64], BF16, kind="ExternalInput")
    d["onesf_d"] = p.dram("onesf", [1, 128], F32, kind="ExternalInput")
    d["cc"] = p.sb("cc_sb", (128, 8), F32)
    d["idn"] = p.sb("idn_sb", (128, 128), BF16)
    d["ones"] = p.sb("ones_sb", (128, 128), BF16)
    d["eye64"] = p.sb("eye64_sb", (128, 64), BF16)
    d["onesf"] = p.sb("onesf_sb", (1, 128), F32)
    p.dma(d["cc"][:], d["cc_d"], q="qs")
    p.dma(d["idn"][:], d["idn_d"], q="qs")
    p.dma(d["ones"][:], d["ones_d"], q="qs")
    p.dma(d["eye64"][:], d["eye64_d"], q="qs")
    p.dma(d["onesf"][:], d["onesf_d"], q="qs")
    d["eps"] = d["cc"][:, 0:1]
    d["eighth"] = d["cc"][:, 1:2]
    d["dequant"] = d["cc"][:, 3:4]
    return d


def load_vec_cols(p, name, vec_dram, n, q="qs"):
    t = p.sb(name, (128, n), F32)
    for i in range(n):
        src = vec_dram[i * 128:(i + 1) * 128].rearrange("(p o) -> p o", o=1)
        p.dma(t[:, i:i + 1], src, q=q)
    return t


class LnScr:
    def __init__(self, p, tag, n=4):
        self.n = n
        self.bst = [p.sb(f"{tag}_bst{i}", (128, 12), F32) for i in range(n)]
        self.agg = [p.sb(f"{tag}_agg{i}", (128, 2), F32) for i in range(n)]
        self.rstd = [p.sb(f"{tag}_rstd{i}", (128, 1), F32) for i in range(n)]
        self.nmu = [p.sb(f"{tag}_nmu{i}", (128, 1), F32) for i in range(n)]
        self.i = 0

    def next(self):
        i = self.i % self.n
        self.i += 1
        return self.bst[i], self.agg[i], self.rstd[i], self.nmu[i]


def ln_stats(p, C, scr, src_tile, ncols=1024):
    bst, agg, rstd, nmu = scr.next()
    nch = (ncols + 511) // 512
    for c in range(nch):
        p.bnstats(bst[:, 6 * c:6 * c + 6], src_tile[:, 512 * c:min(512 * (c + 1), ncols)])
    p.bnaggr(agg[:], bst[:, 0:6 * nch])
    p.act(nmu[:], agg[:, 1:2], AF.Ln, bias=C["eps"])
    p.act(rstd[:], nmu[:], AF.Exp, scale=-0.5)
    p.stt(nmu[:], agg[:, 0:1], -1.0, rstd[:], MULT, MULT)
    return rstd, nmu


def _ap(x):
    return x if isinstance(x, bass.AP) else x[:]


def epilogue_tile(p, C, scr, blk_tile, b, ti, x_dram, xo_dram, xbuf, lnbuf):
    xbuf = _ap(xbuf)
    lnbuf = _ap(lnbuf)
    rstd, nmu = ln_stats(p, C, scr, blk_tile)
    p.act(lnbuf, blk_tile, AF.Identity, bias=nmu[:, 0:1], scale=rstd[:, 0:1])
    p.dma(xbuf, x_dram[b, ti * 128:(ti + 1) * 128, :], q="qs")
    p.tt(xbuf, xbuf, lnbuf, ADD)
    p.dma(xo_dram[b, ti * 128:(ti + 1) * 128, :], xbuf, q="qg")



def declare_block0(p):
    return {
        "w_qkv": p.dram("w_qkv", [1024, 3072], F32, kind="ExternalInput"),
        "b_qkv": p.dram("b_qkv", [3072], F32, kind="ExternalInput"),
        "w_proj": p.dram("w_proj", [1024, 1024], F32, kind="ExternalInput"),
        "b_proj": p.dram("b_proj", [1024], F32, kind="ExternalInput"),
    }


def block0_body(p, C, scr, pf, pb, ins, x_d, xo_d, nbatch=2):
    wqkv_d = ins["w_qkv"]
    bqkv_d = ins["b_qkv"]
    wproj_d = ins["w_proj"]
    bproj_d = ins["b_proj"]

    bq = load_vec_cols(p, "bq", bqkv_d, 24)
    bp = load_vec_cols(p, "bp", bproj_d, 8)

    xnT = p.sb("xnT", (128, 16384), BF16)
    qT = [p.sb(f"qT{j}", (128, 2048), BF16) for j in range(8)]
    kT = [p.sb(f"kT{j}", (128, 2048), BF16) for j in range(8)]
    vtok = p.sb("vtok", (128, 16 * 1040), BF16)
    attnT = xnT
    eS = [p.sb(f"eS{h}", (128, 8 * 512), BF16) for h in range(4)]

    xt = [p.sb(f"xt{i}", (128, 1024), F32) for i in range(2)]
    xnt = [p.sb(f"xnt{i}", (128, 1024), BF16) for i in range(2)]
    xtA = xt + [eS[3][:].bitcast(F32)[:, 0:1024]]
    xntA = xnt + [eS[3][:, 2048:3072]]
    wst = [p.sb(f"wst{i}", (128, 128), F32) for i in range(4)]
    wqb = [p.sb(f"wqb{i}", (128, 128), BF16) for i in range(16)]
    wpb = [p.sb(f"wpb{i}", (128, 1024), BF16) for i in range(8)]

    pjt = [p.sb(f"pjt{i}", (128, 512), BF16) for i in range(2)]
    otokd = [eS[2], eS[3]]
    lnb = [eS[0][:].bitcast(F32)[:, 0:1024], eS[1][:].bitcast(F32)[:, 0:1024]]
    xrb = xt
    avs = [p.sb(f"avs{i}", (64, 512), BF16) for i in range(2)]
    wpf = xt[0][:]
    rden = lnb[0][0:1, 0:512]
    rrow = lnb[1][0:1, 0:512]

    for b in range(nbatch):
        prologue_to_xnT(p, C, scr, x_d, b, xnT, xtA, xntA, pb)

        for ft in range(24):
            wo = 8 * (ft % 2)
            for dt in range(8):
                ws = wst[dt % 4]
                p.dma(ws[:], wqkv_d[dt * 128:(dt + 1) * 128, ft * 128:(ft + 1) * 128], q="qs")
                p.copy(wqb[wo + dt][:], ws[:], eng="v")
            for tc in range(4):
                psq = pf[tc]
                for dt in range(8):
                    p.mm(psq, wqb[wo + dt][:], xnT[:, dt * 2048 + tc * 512: dt * 2048 + (tc + 1) * 512],
                         start=(dt == 0), stop=(dt == 7), commit=(dt == 7))
                if ft < 16:
                    if tc % 2 == 1:
                        dstT = qT[ft] if ft < 8 else kT[ft - 8]
                        p.evac(dstT[:, (tc - 1) * 512:(tc + 1) * 512],
                               p.pw[tc // 2][:], bias=bq[:, ft:ft + 1], eng="v")
                else:
                    vs = pjt[tc % 2]
                    p.evac(vs[:], psq, bias=bq[:, ft:ft + 1], eng="v")
                    hj = ft - 16
                    pbt = pb[tc % 2]
                    for q4 in range(4):
                        p.trans(pbt[:, q4 * 128:(q4 + 1) * 128],
                                vs[:, q4 * 128:(q4 + 1) * 128], C["idn"][:],
                                commit=(q4 == 3))
                    base = tc * 4 * 1040 + (2 * hj) * 65
                    dst = bass.AP(vtok, base, [[16640, 128], [1040, 4], [65, 2], [1, 64]])
                    srcv = pbt[:, 0:512].rearrange("p (q h f) -> p q h f", q=4, h=2)
                    p.copy(dst, srcv, eng="v")
        if b == 0:
            for kt in range(16):
                for h in range(16):
                    p.memset(vtok[:, kt * 1040 + h * 65 + 64: kt * 1040 + h * 65 + 65], 1.0, eng="g")

        for hp in range(8):
            for qc in range(4):
                for kh in range(2):
                    for k8 in range(8):
                        kt = kh * 8 + k8
                        ps1 = pf[k8 % 2]
                        ps2 = pf[2 + k8 % 2]
                        p.mm(ps1, kT[hp][0:64, kt * 128:(kt + 1) * 128],
                             qT[hp][0:64, qc * 512:(qc + 1) * 512], tp=(0, 0))
                        p.mm(ps2, kT[hp][64:128, kt * 128:(kt + 1) * 128],
                             qT[hp][64:128, qc * 512:(qc + 1) * 512], tp=(64, 0))
                        if k8 % 2 == 1:
                            p.act(eS[0 + 2 * kh][:, (k8 - 1) * 512:(k8 + 1) * 512],
                                  p.pw[0][:], AF.Exp, scale=C["eighth"])
                            p.act(eS[1 + 2 * kh][:, (k8 - 1) * 512:(k8 + 1) * 512],
                                  p.pw[1][:], AF.Exp, scale=C["eighth"])
                for kh in range(2):
                    for h01 in range(2):
                        h = hp * 2 + h01
                        psav = pf[4 + h01]
                        for k8 in range(8):
                            kt = kh * 8 + k8
                            p.mm(psav[0:65, :], vtok[:, kt * 1040 + h * 65: kt * 1040 + h * 65 + 65],
                                 eS[h01 + 2 * kh][:, k8 * 512:(k8 + 1) * 512],
                                 start=(kt == 0), stop=(kt == 15), commit=(k8 == 7))
                for h01 in range(2):
                    psav = pf[4 + h01]
                    p.act(rden, psav[64:65, :], AF.Ln)
                    p.act(rrow, rden, AF.Exp, scale=-1.0)
                    psden = pb[h01][:].bitcast(F32)
                    p.mm(psden[0:64, :], C["onesf"][0:1, 0:64], rrow)
                    av = avs[h01]
                    p.copy(av[:], psav[0:64, :], eng="v")
                    p.tt(attnT[64 * h01: 64 * h01 + 64, hp * 2048 + qc * 512: hp * 2048 + (qc + 1) * 512],
                         av[:], psden[0:64, :], MULT)

        for dt in range(8):
            p.dma(wpf, wproj_d[dt * 128:(dt + 1) * 128, :], q="qs")
            p.copy(wpb[dt][:], wpf, eng="v")
        for tc in range(4):
            for ot in range(8):
                psp = pf[ot % 4]
                for dt in range(8):
                    p.mm(psp, wpb[dt][:, ot * 128:(ot + 1) * 128],
                         attnT[:, dt * 2048 + tc * 512: dt * 2048 + (tc + 1) * 512],
                         start=(dt == 0), stop=(dt == 7), commit=(dt == 7))
                pj = pjt[ot % 2]
                p.evac(pj[:], psp, bias=bp[:, ot:ot + 1], eng="v")
                batched_out_tile(p, C, pb, ot, pj[:], otokd[0], ot, eng="v")
            for q4 in range(4):
                ti = tc * 4 + q4
                epilogue_tile(p, C, scr, otokd[0][:, q4 * 1024:(q4 + 1) * 1024], b, ti,
                              x_d, xo_d, xrb[q4 % 2], lnb[q4 % 2])


def common_pools(p):
    C = decl_consts(p)
    scr = LnScr(p, "ln")
    pw = [p.ps(f"pw{i}", (128, 1024), F32) for i in range(2)]
    pnx = p.ps("pnx", (128, 1024), F32)
    pb = [p.ps(f"pb{i}", (128, 1024), BF16) for i in range(2)]
    pf = [pw[0][:, 0:512], pw[0][:, 512:1024], pw[1][:, 0:512], pw[1][:, 512:1024],
          pnx[:, 0:512], pnx[:, 512:1024]]
    p.pw = pw
    p.pnx = pnx
    return C, scr, pf, pb


def build_block0(p, nbatch=2):
    x_d = p.dram("x", [nbatch, 2048, 1024], F32, kind="ExternalInput")
    xo_d = p.dram("xo", [nbatch, 2048, 1024], F32, kind="ExternalOutput")
    ins = declare_block0(p)
    C, scr, pf, pb = common_pools(p)
    p.push()
    block0_body(p, C, scr, pf, pb, ins, x_d, xo_d, nbatch)
    p.pop()
    p.drain()
    return p


def prologue_ti(p, C, scr, x_d, b, xnT, xt, xnt, pb, ti):
    xa = xt[ti % len(xt)]
    xna = xnt[ti % len(xnt)]
    xa_, xna_ = _ap(xa), _ap(xna)
    p.dma(xa_, x_d[b, ti * 128:(ti + 1) * 128, :], q="qs")
    rstd, nmu = ln_stats(p, C, scr, xa_)
    p.act(xna_, xa_, AF.Identity, bias=nmu[:, 0:1], scale=rstd[:, 0:1])
    pbt = pb[ti % 2]
    for j in range(8):
        p.trans(pbt[:, j * 128:(j + 1) * 128], xna[:, j * 128:(j + 1) * 128],
                C["idn"][:], commit=(j == 7))
    dst = bass.AP(xnT.tensor if isinstance(xnT, bass.AP) else xnT,
                  ti * 128, [[16384, 128], [2048, 8], [1, 128]])
    p.copy(dst, pbt[:].rearrange("p (j f) -> p j f", j=8), eng="v")


def prologue_to_xnT(p, C, scr, x_d, b, xnT, xt, xnt, pb):
    for ti in range(16):
        prologue_ti(p, C, scr, x_d, b, xnT, xt, xnt, pb, ti)


def batched_out_tile(p, C, pb, pbi, src_bf16_512, otok, ot, eng="s"):
    pbt = pb[pbi % 2]
    for q4 in range(4):
        p.trans(pbt[:, q4 * 128:(q4 + 1) * 128],
                src_bf16_512[:, q4 * 128:(q4 + 1) * 128], C["idn"][:],
                commit=(q4 == 3))
    dst = bass.AP(otok, ot * 128, [[4096, 128], [1024, 4], [1, 128]])
    p.copy(dst, pbt[:, 0:512].rearrange("p (q f) -> p q f", q=4), eng=eng)


def out_transpose_epilogue(p, C, scr, b, tc, evac_src_psum_fn, nout, x_d, xo_d,
                           pb, pjt, otok, lnb, xrb, pre_add_fn=None):
    for ot in range(nout):
        ps, bias, scale = evac_src_psum_fn(ot)
        pj = pjt[ot % 2]
        kw = {}
        if bias is not None:
            kw["bias"] = bias
        if scale is not None:
            kw["scale"] = scale
        p.act(pj[:], ps, AF.Identity, **kw)
        for q4 in range(4):
            pbt = pb[q4 % 2]
            p.trans(pbt[:, 0:128], pj[:, q4 * 128:(q4 + 1) * 128], C["idn"][:])
            p.copy(otok[q4][:, ot * 128:(ot + 1) * 128], pbt[:, 0:128], eng="s")
    for q4 in range(4):
        ti = tc * 4 + q4
        src = otok[q4][:]
        if pre_add_fn is not None:
            src = pre_add_fn(q4, ti)
        epilogue_tile(p, C, scr, src, b, ti, x_d, xo_d, xrb[q4 % 2], lnb[q4 % 2])



def declare_block1(p):
    return {
        "w_gate": p.dram("w_gate", [3072, 1024], I32, kind="ExternalInput"),
        "w_up": p.dram("w_up", [3072, 1024], I32, kind="ExternalInput"),
        "w_down": p.dram("w_down", [1024, 3072], I32, kind="ExternalInput"),
    }


def block1_body(p, C, scr, pf, pb, ins, x_d, xo_d, nbatch=2):
    wg_d = ins["w_gate"]
    wu_d = ins["w_up"]
    wd_d = ins["w_down"]

    xnT = p.sb("b1xnT", (128, 16384), BF16)
    hb = [p.sb(f"hb{j}", (128, 2048), BF16) for j in range(24)]
    xnt = [hb[23][:, 0:1024], hb[23][:, 1024:2048]]

    xt = [p.sb(f"xt{i}", (128, 1024), F32) for i in range(2)]
    wif = [p.sb(f"wif{i}", (128, 1024), I32) for i in range(2)]
    wib = [p.sb(f"wib{i}", (128, 1024), BF16) for i in range(2)]
    wgT = [p.sb(f"wgT{i}", (128, 128), BF16) for i in range(16)]
    wuT = [p.sb(f"wuT{i}", (128, 128), BF16) for i in range(16)]
    wdT = [p.sb(f"wdT{i}", (128, 128), BF16) for i in range(24)]
    sg1 = p.sb("sg1", (128, 1024), BF16)
    sdT = [p.sb(f"sdT{i}", (128, 2048), BF16) for i in range(8)]
    otok = p.sb("b1otok", (128, 4096), BF16)
    lnb = [xnT[:].bitcast(F32)[:, 0:1024], xnT[:].bitcast(F32)[:, 1024:2048]]
    xrb = xt

    s2col = p.sb("s2col", (128, 1), F32)
    p.act(s2col[:], C["dequant"], AF.Square)

    for b in range(nbatch):
        prologue_to_xnT(p, C, scr, x_d, b, xnT, xt, xnt, pb)

        for ht in range(24):
            wo = 8 * (ht % 2)
            for wk, (w_d, wT) in enumerate(((wg_d, wgT), (wu_d, wuT))):
                wi = wif[wk]
                wb = wib[wk]
                p.dma(wi[:], w_d[ht * 128:(ht + 1) * 128, :], q="qs")
                p.copy(wb[:], wi[:], eng="v")
                for j in range(8):
                    pbt = pb[j % 2]
                    p.trans(pbt[:, 0:128], wb[:, j * 128:(j + 1) * 128], C["idn"][:])
                    p.copy(wT[wo + j][:], pbt[:, 0:128], eng="s")
            for tc in range(4):
                psg = pf[0 + tc % 2]
                psu = pf[2 + tc % 2]
                for dtt in range(8):
                    p.mm(psg, wgT[wo + dtt][:], xnT[:, dtt * 2048 + tc * 512: dtt * 2048 + (tc + 1) * 512],
                         start=(dtt == 0), stop=(dtt == 7), commit=(dtt == 7))
                for dtt in range(8):
                    p.mm(psu, wuT[wo + dtt][:], xnT[:, dtt * 2048 + tc * 512: dtt * 2048 + (tc + 1) * 512],
                         start=(dtt == 0), stop=(dtt == 7), commit=(dtt == 7))
                if tc % 2 == 1:
                    p.act(sg1[:], p.pw[0][:], AF.Silu, scale=C["dequant"])
                    p.tt(hb[ht][:, (tc - 1) * 512:(tc + 1) * 512], sg1[:], p.pw[1][:], MULT)

        for dot in range(8):
            for c3 in range(3):
                wi = wif[c3 % 2]
                wb = wib[c3 % 2]
                p.dma(wi[:], wd_d[dot * 128:(dot + 1) * 128, c3 * 1024:(c3 + 1) * 1024], q="qs")
                p.copy(wb[:], wi[:], eng="v")
                for h8 in range(8):
                    ht = c3 * 8 + h8
                    pbt = pb[ht % 2]
                    p.trans(pbt[:, 0:128], wb[:, h8 * 128:(h8 + 1) * 128], C["idn"][:])
                    p.copy(wdT[ht][:], pbt[:, 0:128], eng="s")
            for tc in range(4):
                psd = pf[4 + tc % 2]
                for ht in range(24):
                    p.mm(psd, wdT[ht][:], hb[ht][:, tc * 512:(tc + 1) * 512],
                         start=(ht == 0), stop=(ht == 23), commit=(ht == 23))
                p.evac(sdT[dot][:, tc * 512:(tc + 1) * 512], psd,
                       scale=s2col[:, 0:1], eng="v")
        for tc in range(4):
            for ot in range(8):
                batched_out_tile(p, C, pb, ot, sdT[ot][:, tc * 512:(tc + 1) * 512],
                                 otok, ot, eng="v")
            for q4 in range(4):
                ti = tc * 4 + q4
                epilogue_tile(p, C, scr, otok[:, q4 * 1024:(q4 + 1) * 1024], b, ti,
                              x_d, xo_d, xrb[q4 % 2], lnb[q4 % 2])


def build_block1(p, nbatch=2):
    x_d = p.dram("x", [nbatch, 2048, 1024], F32, kind="ExternalInput")
    xo_d = p.dram("xo", [nbatch, 2048, 1024], F32, kind="ExternalOutput")
    ins = declare_block1(p)
    C, scr, pf, pb = common_pools(p)
    p.push()
    block1_body(p, C, scr, pf, pb, ins, x_d, xo_d, nbatch)
    p.pop()
    p.drain()
    return p



def declare_block3(p):
    return {
        "weight": p.dram("weight", [1024, 1024], F32, kind="ExternalInput"),
        "disha": p.dram("disha", [64, 1024], F32, kind="ExternalInput"),
    }


def block3_body(p, C, scr, pf, pb, ins, x_d, xo_d, nbatch=2):
    w_d = ins["weight"]
    dsh_d = ins["disha"]

    xnT = p.sb("b3xnT", (128, 16384), BF16)
    xnT2 = p.sb("b3xnT2", (128, 16384), BF16)
    xt = [p.sb(f"xt{i}", (128, 1024), F32) for i in range(2)]
    xnt = [p.sb(f"xnt{i}", (128, 1024), BF16) for i in range(2)]
    xt2 = [p.sb(f"x2t{i}", (128, 1024), F32) for i in range(2)]
    xnt2 = [p.sb(f"x2nt{i}", (128, 1024), BF16) for i in range(2)]
    scr2 = LnScr(p, "ln3b")
    wf = p.sb("wf", (128, 1024), F32)
    wb = p.sb("wb", (128, 1024), BF16)
    wT = [p.sb(f"wT{i}", (128, 1024), BF16) for i in range(8)]
    dsf = p.sb("dsf", (64, 1024), F32)
    dsb = p.sb("dsb", (64, 1024), BF16)
    poolb = p.sb("poolb", (64, 512), BF16)
    sdT = [p.sb(f"sdT{i}", (128, 2048), BF16) for i in range(8)]
    otok = p.sb("b3otok", (128, 4096), BF16)
    lnb = [p.sb(f"lnb{i}", (128, 1024), F32) for i in range(1)]
    xrb = xt

    for ot in range(8):
        p.dma(wf[:], w_d[ot * 128:(ot + 1) * 128, :], q="qs")
        p.copy(wb[:], wf[:], eng="v")
        for dt in range(8):
            pbt = pb[dt % 2]
            p.trans(pbt[:, 0:128], wb[:, dt * 128:(dt + 1) * 128], C["idn"][:])
            p.copy(wT[dt][:, ot * 128:(ot + 1) * 128], pbt[:, 0:128], eng="s")
    p.dma(dsf[:], dsh_d, q="qs")
    p.copy(dsb[:], dsf[:], eng="v")

    xnTs = [xnT, xnT2]
    prologue_to_xnT(p, C, scr, x_d, 0, xnT, xt, xnt, pb)
    for b in range(nbatch):
        xnTb = xnTs[b % 2]
        for tc in range(4):
            psp = pf[4]
            for dt in range(8):
                p.mm(psp[0:64, :], C["eye64"][:],
                     xnTb[:, dt * 2048 + tc * 512: dt * 2048 + (tc + 1) * 512],
                     start=(dt == 0), stop=(dt == 7), commit=(dt == 7))
            p.act(poolb[:], psp[0:64, :], AF.Identity)
            for ot in range(8):
                psr = pf[ot % 4]
                for dt in range(8):
                    p.mm(psr, wT[dt][:, ot * 128:(ot + 1) * 128],
                         xnTb[:, dt * 2048 + tc * 512: dt * 2048 + (tc + 1) * 512],
                         start=(dt == 0), stop=False, commit=False)
                p.mm(psr, dsb[0:64, ot * 128:(ot + 1) * 128], poolb[:],
                     start=False, stop=True, commit=True)
                p.act(sdT[ot][:, tc * 512:(tc + 1) * 512], psr, AF.Identity)
            for ot in range(8):
                batched_out_tile(p, C, pb, ot, sdT[ot][:, tc * 512:(tc + 1) * 512],
                                 otok, ot)
            for q4 in range(4):
                ti = tc * 4 + q4
                epilogue_tile(p, C, scr, otok[:, q4 * 1024:(q4 + 1) * 1024], b, ti,
                              x_d, xo_d, xrb[q4 % 2], lnb[0])
                if b + 1 < nbatch:
                    prologue_ti(p, C, scr2, x_d, b + 1, xnTs[(b + 1) % 2],
                                xt2, xnt2, pb, ti)


def build_block3(p, nbatch=2):
    x_d = p.dram("x", [nbatch, 2048, 1024], F32, kind="ExternalInput")
    xo_d = p.dram("xo", [nbatch, 2048, 1024], F32, kind="ExternalOutput")
    ins = declare_block3(p)
    C, scr, pf, pb = common_pools(p)
    p.push()
    block3_body(p, C, scr, pf, pb, ins, x_d, xo_d, nbatch)
    p.pop()
    p.drain()
    return p



def declare_block2(p, nbatch=2):
    return {
        "ln_g": p.dram("ln_g", [1024], F32, kind="ExternalInput"),
        "ln_b": p.dram("ln_b", [1024], F32, kind="ExternalInput"),
        "W_in": p.dram("W_in", [1024, 4096], F32, kind="ExternalInput"),
        "conv_w": p.dram("conv_w", [2048, 4], F32, kind="ExternalInput"),
        "conv_b": p.dram("conv_b", [2048], F32, kind="ExternalInput"),
        "W_x": p.dram("W_x", [2048, 96], F32, kind="ExternalInput"),
        "W_dt": p.dram("W_dt", [64, 2048], F32, kind="ExternalInput"),
        "b_dt": p.dram("b_dt", [2048], F32, kind="ExternalInput"),
        "A_log": p.dram("A_log", [2048, 16], F32, kind="ExternalInput"),
        "D": p.dram("D", [2048], F32, kind="ExternalInput"),
        "W_out": p.dram("W_out", [2048, 1024], F32, kind="ExternalInput"),
        "xin_s": p.dram("xin_s", [nbatch, 2048, 1024], BF16),
        "dt_sc": p.dram("dt_sc", [nbatch, 16, 128, 2048], BF16),
        "z_s": p.dram("z_s", [nbatch, 16, 128, 2048], BF16),
        "y_s": p.dram("y_s", [nbatch, 16, 128, 4096], BF16),
        "xh_s": p.dram("xh_s", [nbatch, 16, 128, 2048], BF16),
        "xc_s": p.dram("xc_s", [nbatch, 16, 128, 2048], BF16),
        "bc_s": p.dram("bc_s", [32, 2048], BF16),
    }


def block2_params(p, ins):
    P2 = {}
    P2["lg"] = load_vec_cols(p, "lg", ins["ln_g"], 8, q="qg")
    P2["lb"] = load_vec_cols(p, "lb", ins["ln_b"], 8, q="qg")
    P2["cb"] = load_vec_cols(p, "cbv", ins["conv_b"], 16, q="qg")
    P2["bdt"] = load_vec_cols(p, "bdtv", ins["b_dt"], 16, q="qg")
    P2["dcol"] = load_vec_cols(p, "dcol", ins["D"], 16, q="qg")

    cw = p.sb("cw", (128, 64), F32)
    A_sb = p.sb("A_sb", (128, 256), F32)
    astg = p.sb("astg", (128, 16), F32)
    for dt in range(16):
        p.dma(cw[:, dt * 4:(dt + 1) * 4], ins["conv_w"][dt * 128:(dt + 1) * 128, :], q="qg")
        p.dma(astg[:], ins["A_log"][dt * 128:(dt + 1) * 128, :], q="qg")
        p.act(A_sb[:, dt * 16:(dt + 1) * 16], astg[:], AF.Exp)
        p.ts(A_sb[:, dt * 16:(dt + 1) * 16], A_sb[:, dt * 16:(dt + 1) * 16], -1.0, MULT)
    P2["cw"] = cw
    P2["A_sb"] = A_sb
    return P2


def block2_body(p, C, scr, pf, pb, ins, x_d, xo_d, nbatch=2, P2=None):
    win_d = ins["W_in"]
    wout_d = ins["W_out"]
    xin_s = ins["xin_s"]; z_s = ins["z_s"]; y_s = ins["y_s"]; dt_sc = ins["dt_sc"]
    xh_s = ins["xh_s"]; xc_s = ins["xc_s"]

    if P2 is None:
        P2 = block2_params(p, ins)
    lg = P2["lg"]; lb = P2["lb"]; cb = P2["cb"]; bdt = P2["bdt"]
    dcol = P2["dcol"]; cw = P2["cw"]; A_sb = P2["A_sb"]

    Wx_sb = p.sb("Wx_sb", (128, 16 * 96), BF16)
    wxf = p.sb("wxf", (128, 96), F32)
    for dt in range(16):
        p.dma(wxf[:], ins["W_x"][dt * 128:(dt + 1) * 128, :], q="qs")
        p.copy(Wx_sb[:, dt * 96:(dt + 1) * 96], wxf[:], eng="v")

    wdtstg = p.sb("wdtstg", (64, 1024), F32)
    Wdt_sb = p.sb("Wdt_sb", (64, 2048), BF16)
    for h in range(2):
        p.dma(wdtstg[:], ins["W_dt"][:, h * 1024:(h + 1) * 1024], q="qs")
        p.copy(Wdt_sb[:, h * 1024:(h + 1) * 1024], wdtstg[:], eng="v")

    bfB = [p.sb(f"bfB{i}", (128, 2048), BF16) for i in range(8)]
    bfC = [p.sb(f"bfC{i}", (128, 2048), BF16) for i in range(8)]
    w8 = [p.sb(f"w8_{i}", (128, 2048), BF16) for i in range(8)]
    xcs = [p.sb(f"xcs{i}", (128, 2052), BF16) for i in range(2)]
    fA = [p.sb(f"fA{i}", (128, 2048), F32) for i in range(3)]
    yq = [p.sb(f"yq{i}", (128, 2048), BF16) for i in range(2)]
    fB = [p.sb(f"fB{i}", (128, 2048), BF16) for i in range(6)]
    at3 = p.sb("at3", (128, 2048), BF16)
    hcb = [fB[5], fB[5]]
    du2 = [fB[0], p.sb("du1", (128, 2048), BF16)]
    dbc = p.sb("dbc", (96, 2048), BF16)
    brow = [wdtstg[0:1, :].bitcast(BF16)]
    wpl = bfB + w8

    xt = [fA[0][:, 0:1024], fA[1][:, 0:1024], fA[2][:, 0:1024], fA[2][:, 1024:2048]]
    xnt = [fB[1][:, 0:1024], fB[1][:, 1024:2048], fB[2][:, 0:1024], fB[2][:, 1024:2048]]
    xint = [p.sb(f"xint{i}", (128, 1024), BF16) for i in range(2)]
    xint3 = xint + [at3[:, 0:1024], at3[:, 1024:2048]]
    wst = [fB[5][:].bitcast(F32)[:, 128 * i:128 * (i + 1)] for i in range(4)]
    wsb = [fB[2][:, 128 * i:128 * (i + 1)] for i in range(8)] + [fB[3][:, 128 * i:128 * (i + 1)] for i in range(8)]
    zst = [fB[0][:, 0:512], fB[0][:, 512:1024]]
    pjt = [fB[0][:, 1024:1536], fB[0][:, 1536:2048]]

    for b in range(nbatch):
        for ti in range(16):
            xa = xt[ti % 4]
            xna = xnt[ti % 4]
            xia = xint3[ti % 4]
            p.dma(xa, x_d[b, ti * 128:(ti + 1) * 128, :], q="qs")
            rstd, nmu = ln_stats(p, C, scr, xa)
            p.act(xia[:], xa, AF.Identity, bias=nmu[:, 0:1], scale=rstd[:, 0:1])
            p.dma(xin_s[b, ti * 128:(ti + 1) * 128, :], xia[:], q="qg")
            rstd, nmu = ln_stats(p, C, scr, xia[:])
            p.act(xna, xia[:], AF.Identity, bias=nmu[:, 0:1], scale=rstd[:, 0:1])
            for j in range(8):
                pbt = pb[j % 2]
                p.trans(pbt[:, 0:128], xna[:, j * 128:(j + 1) * 128], C["idn"][:])
                p.act(bfB[j][:, ti * 128:(ti + 1) * 128], pbt[:, 0:128], AF.Identity,
                      scale=lg[:, j:j + 1], bias=lb[:, j:j + 1])

        for ot in range(32):
            wo = 8 * (ot % 2)
            for dt8 in range(8):
                ws = wst[dt8 % 4]
                p.dma(ws, win_d[dt8 * 128:(dt8 + 1) * 128, ot * 128:(ot + 1) * 128], q="qs")
                p.copy(wsb[wo + dt8], ws, eng="v")
            for tc in range(4):
                ps = pf[tc]
                for dt8 in range(8):
                    p.mm(ps, wsb[wo + dt8], bfB[dt8][:, tc * 512:(tc + 1) * 512],
                         start=(dt8 == 0), stop=(dt8 == 7), commit=(dt8 == 7))
                if tc % 2 == 1:
                    if ot < 16:
                        xq = yq[ot % 2]
                        p.act(xq[:, (tc - 1) * 512:(tc + 1) * 512],
                              p.pw[tc // 2][:], AF.Identity)
                        if tc == 3:
                            p.dma(xh_s[b, ot], xq[:], q="qg")
                    else:
                        zt = fB[0][:, 0:1024] if tc == 1 else fB[0][:, 1024:2048]
                        p.act(zt, p.pw[tc // 2][:], AF.Identity)
                        p.dma(z_s[b, ot - 16, :, (tc - 1) * 512:(tc + 1) * 512], zt, q="qg")

        psdc = [pf[4], pf[5], pb[0][:].bitcast(F32), pb[1][:].bitcast(F32)]
        for dt in range(16):
            xc = xcs[dt % 2]
            p.dma(xc[:, 3:2051], xh_s[b, dt], q="qs")
            c0 = fA[0] if dt % 2 == 0 else fA[2]
            c1 = fA[1] if dt % 2 == 0 else fA[0]
            p.memset(xc[:, 0:3], 0, eng="g")
            p.ts(c0[:], xc[:, 3:2051], cw[:, dt * 4 + 3: dt * 4 + 4], MULT,
                 s2=cb[:, dt:dt + 1], op1=ADD)
            p.stt(c1[:], xc[:, 2:2050], cw[:, dt * 4 + 2: dt * 4 + 3], c0[:], MULT, ADD)
            p.stt(c0[:], xc[:, 1:2049], cw[:, dt * 4 + 1: dt * 4 + 2], c1[:], MULT, ADD)
            p.stt(c1[:], xc[:, 0:2048], cw[:, dt * 4 + 0: dt * 4 + 1], c0[:], MULT, ADD)
            p.act(xc[:, 3:2051], c1[:], AF.Silu)
            p.dma(xc_s[b, dt], xc[:, 3:2051], q="qg")
            for c in range(4):
                p.mm(psdc[c][0:96, :], Wx_sb[:, dt * 96:(dt + 1) * 96],
                     xc[:, 3 + c * 512: 3 + (c + 1) * 512],
                     start=(dt == 0), stop=(dt == 15), commit=(c == 3))
        for c in range(4):
            p.copy(dbc[64:96, c * 512:(c + 1) * 512], psdc[c][64:96, :], eng="s")
        p.dma(ins["bc_s"], dbc[64:96, :], q="qg")
        for c in range(4):
            p.copy(dbc[0:64, c * 512:(c + 1) * 512], psdc[c][0:64, :], eng="s")

        yacc = [pf[4], pf[5], pb[0][:].bitcast(F32), pb[1][:].bitcast(F32)]
        dtb = [fA[0][:].bitcast(BF16)[:, 0:2048], fA[0][:].bitcast(BF16)[:, 2048:4096]]
        aTb = [fA[2][:].bitcast(BF16)[:, 0:2048], fA[2][:].bitcast(BF16)[:, 2048:4096],
               at3[:]]
        def bcast_bc(nh, n8, which):
            roff, dest, q = (0, bfB, "qs") if which == 0 else (16, bfC, "qg")
            src = bass.AP(ins["bc_s"].tensor, (roff + nh * 8 + n8) * 2048,
                          [[0, 128], [1, 2048]])
            p.dma(dest[n8][:], src, q=q)

        for nh in range(2):
            if nh == 0:
                for n8 in range(8):
                    bcast_bc(0, n8, 0)
                    bcast_bc(0, n8, 1)
            if nh == 0:
                p.dma(xcs[0][:, 3:2051], xc_s[b, 0], q="qs")

            def emit_head(dtn, stage):
                if dtn >= 16:
                    return
                dtcn = dtb[dtn % 2]
                if nh == 0:
                    if stage in (0, 1, 2, 3) and stage % 2 == 0:
                        for c in (stage, stage + 1):
                            psq = pf[2 + c % 2]
                            p.mm(psq[:], Wdt_sb[0:64, dtn * 128:(dtn + 1) * 128],
                                 dbc[0:64, c * 512:(c + 1) * 512])
                    elif stage in (1, 3):
                        for c in (stage - 1, stage):
                            psq = pf[2 + c % 2]
                            p.act(fA[1][:, c * 512:(c + 1) * 512], psq[:], AF.Exp,
                                  bias=bdt[:, dtn:dtn + 1])
                    elif stage == 4:
                        p.act(dtcn, fA[1][:], AF.Ln, bias=C["cc"][:, 2:3])
                        p.dma(dt_sc[b, dtn], dtcn, q="qg")
                    else:
                        p.tt(du2[dtn % 2][:], dtcn, xcs[dtn % 2][:, 3:2051],
                             MULT, eng="g")
                else:
                    if stage == 0:
                        p.dma(dtcn, dt_sc[b, dtn], q="qs")
                        p.dma(yq[dtn % 2][:], y_s[b, dtn, :, 0:2048], q="qs")
                    elif stage == 5:
                        p.tt(du2[dtn % 2][:], dtcn, xcs[dtn % 2][:, 3:2051],
                             MULT, eng="g")

            for st in range(6):
                if nh == 1 and st == 0:
                    continue
                emit_head(0, st)
            tail_prev = None
            for dt in range(16):
                dtc = dtb[dt % 2]
                xc = xcs[dt % 2]
                du = du2[dt % 2]

                def hc_emit(n8):
                    hT_ = fB[3 + n8 % 2]
                    hc = hcb[n8 % 2]
                    p.tt(hc[:], hT_[:], bfC[n8][:], MULT, eng="v")
                    for c in range(4):
                        p.mm(yacc[c], C["idn"][:], hc[:, c * 512:(c + 1) * 512],
                             start=(nh == 0 and n8 == 0), stop=(n8 == 7), commit=(c == 3))

                for n8 in range(8):
                    nn = nh * 8 + n8
                    aT = aTb[n8 % 3]
                    vT = fB[1 + n8 % 2]
                    hT = fB[3 + n8 % 2]
                    p.act(aT, dtc, AF.Exp, scale=A_sb[:, dt * 16 + nn: dt * 16 + nn + 1])
                    p.tt(vT[:], du[:], bfB[n8][:], MULT, eng="v")
                    p.scan(hT[:], aT, vT[:], 0.0)
                    if nh == 0 and dt == 15:
                        bcast_bc(1, n8, 0)
                        if n8 == 4:
                            p.dma(xcs[0][:, 3:2051], xc_s[b, 0], q="qs")
                            p.dma(dtb[0], dt_sc[b, 0], q="qs")
                            p.dma(yq[0][:], y_s[b, 0, :, 0:2048], q="qs")
                    if n8 == 0 and tail_prev is not None:
                        tail_prev()
                    if n8 == 0 and nh == 1:
                        y0s = yq[dt % 2]
                        for c in range(4):
                            p.mm(yacc[c], C["idn"][:], y0s[:, c * 512:(c + 1) * 512],
                                 start=True, stop=False, commit=(c == 3))
                    if n8 == 1 and dt + 1 < 16:
                        p.dma(xcs[(dt + 1) % 2][:, 3:2051], xc_s[b, dt + 1], q="qs")
                    if n8 >= 1:
                        hc_emit(n8 - 1)
                        if nh == 0 and dt == 15:
                            bcast_bc(1, n8 - 1, 1)
                    if 1 <= n8 <= 4:
                        emit_head(dt + 1, n8 - 1)
                    elif n8 == 5:
                        emit_head(dt + 1, 4)
                    elif n8 == 6:
                        emit_head(dt + 1, 5)
                hc_emit(7)
                if nh == 0 and dt == 15:
                    bcast_bc(1, 7, 1)

                def make_tail(dtp, xcp, nhp):
                    def tail():
                        if nhp == 0:
                            yo = yq[dtp % 2]
                            p.act(yo[:, 0:1024], p.pnx[:, 0:1024], AF.Identity)
                            for c in (2, 3):
                                p.act(yo[:, c * 512:(c + 1) * 512], yacc[c], AF.Identity)
                            p.dma(y_s[b, dtp, :, 0:2048], yo[:], q="qg")
                        else:
                            p.stt(at3[:, 0:1024], xcp[:, 3:1027],
                                  dcol[:, dtp:dtp + 1], p.pnx[:, 0:1024], MULT, ADD)
                            for c in (2, 3):
                                p.stt(at3[:, c * 512:(c + 1) * 512],
                                      xcp[:, 3 + c * 512: 3 + (c + 1) * 512],
                                      dcol[:, dtp:dtp + 1], yacc[c], MULT, ADD)
                            p.dma(y_s[b, dtp, :, 2048:4096], at3[:], q="qg")
                    return tail

                tail_prev = make_tail(dt, xc, nh)
            tail_prev()
            tail_prev = None

        for dt in range(16):
            yb = fB[1 + dt % 2]
            zb = fB[3 + dt % 2]
            p.dma(yb[:], y_s[b, dt, :, 2048:4096], q="qs")
            p.dma(zb[:], z_s[b, dt], q="qs")
            p.act(hcb[dt % 2][:], zb[:], AF.Silu)
            p.tt(wpl[dt][:], yb[:], hcb[dt % 2][:], MULT)
            wos = fA[0][:, (dt % 2) * 1024:(dt % 2) * 1024 + 1024]
            p.dma(wos, wout_d[dt * 128:(dt + 1) * 128, :], q="qg")
            p.copy(bfC[dt // 2][:, (dt % 2) * 1024:(dt % 2 + 1) * 1024],
                   wos, eng="v")
        for tc in range(4):
            for ot in range(8):
                ps = pf[ot % 2]
                for dt in range(16):
                    p.mm(ps[:], bfC[dt // 2][:, (dt % 2) * 1024 + ot * 128:
                                             (dt % 2) * 1024 + (ot + 1) * 128],
                         wpl[dt][:, tc * 512:(tc + 1) * 512],
                         start=(dt == 0), stop=(dt == 15), commit=(dt == 15))
                pj = pjt[ot % 2]
                p.act(pj, ps[:], AF.Identity)
                for q4 in range(4):
                    pbt = pb[q4 % 2]
                    p.trans(pbt[:, 0:128], pj[:, q4 * 128:(q4 + 1) * 128], C["idn"][:])
                    p.copy(fB[2 + q4 // 2][:, (q4 % 2) * 1024 + ot * 128:
                                           (q4 % 2) * 1024 + (ot + 1) * 128],
                           pbt[:, 0:128], eng="s")
            for q4 in range(4):
                ti = tc * 4 + q4
                mo = fB[2 + q4 // 2][:, (q4 % 2) * 1024:(q4 % 2) * 1024 + 1024]
                xia = xint[q4 % 2]
                p.dma(xia[:], xin_s[b, ti * 128:(ti + 1) * 128, :], q="qs")
                mtile = fA[2][:, 1024 * (q4 % 2):1024 * (q4 % 2) + 1024]
                p.tt(mtile, mo, xia[:], ADD)
                epilogue_tile(p, C, scr, mtile, b, ti, x_d, xo_d,
                              fA[0][:, 1024 * (q4 % 2):1024 * (q4 % 2) + 1024],
                              yq[q4 % 2][:].bitcast(F32)[:, 0:1024])


def build_block2(p, nbatch=2):
    x_d = p.dram("x", [nbatch, 2048, 1024], F32, kind="ExternalInput")
    xo_d = p.dram("xo", [nbatch, 2048, 1024], F32, kind="ExternalOutput")
    ins = declare_block2(p, nbatch)
    C, scr, pf, pb = common_pools(p)
    p.push()
    block2_body(p, C, scr, pf, pb, ins, x_d, xo_d, nbatch)
    p.pop()
    p.drain()
    return p


def build_fused(p, nbatch=2):
    x_d = p.dram("x", [nbatch, 2048, 1024], F32, kind="ExternalInput")
    xo_d = p.dram("xo", [nbatch, 2048, 1024], F32, kind="ExternalOutput")
    x1_d = p.dram("x1s", [nbatch, 2048, 1024], F32)
    x2_d = p.dram("x2s", [nbatch, 2048, 1024], F32)
    x3_d = p.dram("x3s", [nbatch, 2048, 1024], F32)
    ins0 = declare_block0(p)
    ins1 = declare_block1(p)
    ins2 = declare_block2(p, nbatch)
    ins3 = declare_block3(p)
    C, scr, pf, pb = common_pools(p)
    P2 = block2_params(p, ins2)
    p.push(); block0_body(p, C, scr, pf, pb, ins0, x_d, x1_d, nbatch); p.pop()
    p.barrier()
    p.push(); block1_body(p, C, scr, pf, pb, ins1, x1_d, x2_d, nbatch); p.pop()
    p.barrier()
    p.push(); block2_body(p, C, scr, pf, pb, ins2, x2_d, x3_d, nbatch, P2=P2); p.pop()
    p.barrier()
    p.push(); block3_body(p, C, scr, pf, pb, ins3, x3_d, xo_d, nbatch); p.pop()
    p.drain()
    return p



N_CORES = 8


def kernel(**inputs):
    consts = host_consts()
    xs = np.ascontiguousarray(inputs["L0_attn_x"])
    shards = [xs[2 * c:2 * c + 2] for c in range(N_CORES)]

    p = P()
    build_fused(p, nbatch=2)
    with p.es:
        im = []
        for c in range(N_CORES):
            d = dict(consts)
            d.update({
                "x": shards[c],
                "w_qkv": inputs["L0_attn_w_qkv"],
                "b_qkv": inputs["L0_attn_b_qkv"],
                "w_proj": inputs["L0_attn_w_proj"],
                "b_proj": inputs["L0_attn_b_proj"],
                "w_gate": inputs["L0_ffn_w_gate"],
                "w_up": inputs["L0_ffn_w_up"],
                "w_down": inputs["L0_ffn_w_down"],
                "ln_g": inputs["L1_attn_ln_g"], "ln_b": inputs["L1_attn_ln_b"],
                "W_in": inputs["L1_attn_W_in"], "conv_w": inputs["L1_attn_conv_w"],
                "conv_b": inputs["L1_attn_conv_b"], "W_x": inputs["L1_attn_W_x"],
                "W_dt": inputs["L1_attn_W_dt"], "b_dt": inputs["L1_attn_b_dt"],
                "A_log": inputs["L1_attn_A_log"], "D": inputs["L1_attn_D"],
                "W_out": inputs["L1_attn_W_out"],
                "weight": inputs["L1_ffn_weight"],
                "disha": inputs["L1_ffn_disha"],
            })
            im.append(d)
        res = run_bass_kernel_spmd(p.nc, im, core_ids=list(range(N_CORES)))
    return np.concatenate([r["xo"] for r in res.results], axis=0).astype(np.float32)
